# version 13
# speedup vs baseline: 1.1494x; 1.0106x over previous
import numpy as np
import ml_dtypes
from contextlib import ExitStack

import concourse.bass as bass
import concourse.mybir as mybir
from concourse.bass_utils import run_bass_kernel_spmd

F32 = mybir.dt.float32
BF16 = mybir.dt.bfloat16
AF = mybir.ActivationFunctionType
ALU = mybir.AluOpType
AX = mybir.AxisListType

T_FULL = 4096
N_CORES = 8


def tile_lhsT(W, m_w):
    K, M = W.shape
    assert K % 128 == 0 and M % m_w == 0
    return np.ascontiguousarray(
        W.reshape(K // 128, 128, M // m_w, m_w).transpose(2, 1, 0, 3)
        .reshape(M // m_w, 128, (K // 128) * m_w))


def chunk_bias(b):
    return np.ascontiguousarray(np.asarray(b, dtype=np.float32).reshape(-1, 128).T)


def row_bias(b):
    return np.ascontiguousarray(np.asarray(b, dtype=np.float32).reshape(1, -1))


def alibi_slopes(num_heads):
    base = (2.0 ** 8) ** (1.0 / num_heads)
    return np.asarray([1.0 / base ** (i + 1) for i in range(num_heads)], dtype=np.float32)


def host_prep(inputs, T):
    d = {}
    f32 = lambda a: np.ascontiguousarray(np.asarray(a), dtype=np.float32)

    d["c_ident"] = np.eye(128, dtype=np.float32)
    d["c_ones_col"] = np.ones((128, 1), dtype=np.float32)
    d["c_ones_row"] = np.ones((1, 512), dtype=np.float32)
    d["c_ones_row_f32"] = np.ones((1, 128), dtype=np.float32)
    m = alibi_slopes(16)
    nkt = T // 128
    tab = np.zeros((128, 16 * nkt), dtype=np.float32)
    jj = np.arange(128, dtype=np.float32)[:, None]
    for h in range(16):
        for kt in range(nkt):
            tab[:, h * nkt + kt:h * nkt + kt + 1] = m[h] * (kt * 128 + jj - (T - 1))
    d["c_alibi"] = tab

    for nm in ["Wq", "Wk", "Wv"]:
        d[f"l0a_{nm}"] = tile_lhsT(f32(inputs[f"L0_attn_{nm}"]).T, 256)
    d["l0a_Wo"] = tile_lhsT(f32(inputs["L0_attn_Wo"]).T, 512)
    d["l0a_bq"] = chunk_bias(inputs["L0_attn_bq"])
    d["l0a_bk"] = chunk_bias(inputs["L0_attn_bk"])
    d["l0a_bv"] = row_bias(inputs["L0_attn_bv"])
    d["l0a_bo"] = row_bias(inputs["L0_attn_bo"])

    d["l1a_W"] = tile_lhsT(f32(inputs["L1_attn_W_kqv"]), 256)

    d["l1f_rwT"] = tile_lhsT(np.ascontiguousarray(f32(inputs["L1_ffn_router_weight"]).T), 32)
    d["l1f_ebr"] = np.ascontiguousarray(
        np.repeat(f32(inputs["L1_ffn_e_score_bias"]).reshape(1, 32), 128, axis=0))
    gup = f32(inputs["L1_ffn_gate_up_proj"])
    d["l1f_gup"] = np.stack([tile_lhsT(np.ascontiguousarray(gup[e].T), 1024)[0]
                             for e in range(32)])
    dwn = f32(inputs["L1_ffn_down_proj"])
    d["l1f_down"] = np.stack([tile_lhsT(np.ascontiguousarray(dwn[e].T), 1024)[0]
                              for e in range(32)])
    d["l1f_shg"] = tile_lhsT(np.ascontiguousarray(f32(inputs["L1_ffn_shared_gate_w"]).T), 1024)
    d["l1f_shu"] = tile_lhsT(np.ascontiguousarray(f32(inputs["L1_ffn_shared_up_w"]).T), 1024)
    d["l1f_shd"] = tile_lhsT(np.ascontiguousarray(f32(inputs["L1_ffn_shared_down_w"]).T), 512)
    ind32 = np.zeros((32, 32, 128), dtype=np.float32)
    for e in range(32):
        ind32[e, e, :] = 1.0
    d["c_ind32"] = np.ascontiguousarray(ind32.transpose(1, 0, 2).reshape(32, 32 * 128))

    d["l0f_w1"] = tile_lhsT(f32(inputs["L0_ffn_cn_w1"]), 512)
    d["l0f_w2"] = tile_lhsT(f32(inputs["L0_ffn_cn_w2"]), 512)
    d["l0f_w3"] = tile_lhsT(f32(inputs["L0_ffn_cn_w3"]), 256)
    w4 = f32(inputs["L0_ffn_cn_w4"]).reshape(2, 128, 1)
    d["l0f_w4r"] = np.ascontiguousarray(np.repeat(w4, 128, axis=2))
    d["l0f_b1"] = chunk_bias(inputs["L0_ffn_cn_b1"])
    d["l0f_b2"] = chunk_bias(inputs["L0_ffn_cn_b2"])
    d["l0f_b3"] = chunk_bias(inputs["L0_ffn_cn_b3"])
    d["l0f_b4r"] = np.ascontiguousarray(
        np.repeat(f32(inputs["L0_ffn_cn_b4"]).reshape(1, 1), 128, axis=0))
    d["l0f_gw1"] = tile_lhsT(f32(inputs["L0_ffn_gc_w1"]), 512)
    d["l0f_gw2"] = tile_lhsT(f32(inputs["L0_ffn_gc_w2"]), 256)
    d["l0f_gw3"] = chunk_bias(inputs["L0_ffn_gc_w3"])
    d["l0f_gb1"] = chunk_bias(inputs["L0_ffn_gc_b1"])
    d["l0f_gb2"] = chunk_bias(inputs["L0_ffn_gc_b2"])
    d["l0f_gb3"] = f32(inputs["L0_ffn_gc_b3"]).reshape(1, 1)
    d["l0f_ar"] = f32(inputs["L0_ffn_adaptive_range"]).reshape(1, 1)

    d["l2f_WgT"] = tile_lhsT(f32(inputs["L2_ffn_Wg"]), 8)
    d["l2f_W1"] = np.stack([tile_lhsT(f32(inputs["L2_ffn_W1"][e]), 512)
                            for e in range(8)])
    d["l2f_W2"] = np.stack([tile_lhsT(f32(inputs["L2_ffn_W2"][e]), 128)
                            for e in range(8)]).reshape(8, 8, 128, 2, 2048) \
        .transpose(0, 1, 3, 2, 4).copy()
    d["l2f_Ws1"] = tile_lhsT(f32(inputs["L2_ffn_Ws1"]), 512)
    d["l2f_Ws2"] = tile_lhsT(f32(inputs["L2_ffn_Ws2"]), 128) \
        .reshape(8, 128, 2, 2048).transpose(0, 2, 1, 3).copy()
    b1c = np.zeros((128, 9 * 32), dtype=np.float32)
    for e in range(8):
        b1c[:, e * 32:(e + 1) * 32] = chunk_bias(inputs["L2_ffn_b1"][e])
    b1c[:, 8 * 32:9 * 32] = chunk_bias(inputs["L2_ffn_bs1"])
    d["l2f_b1c"] = b1c
    b2cat = np.zeros((1, 9 * 1024), dtype=np.float32)
    b2cat[0, :8 * 1024] = f32(inputs["L2_ffn_b2"]).reshape(-1)
    b2cat[0, 8 * 1024:] = f32(inputs["L2_ffn_bs2"])
    d["l2f_b2cat"] = b2cat
    ind8 = np.zeros((8, 8, 128), dtype=np.float32)
    for e in range(8):
        ind8[e, e, :] = 1.0
    d["c_ind8"] = np.ascontiguousarray(ind8.transpose(1, 0, 2).reshape(8, 8 * 128))
    d["c_tenth"] = np.full((1, 512), 0.1, dtype=np.float32)

    d["l2a_W"] = tile_lhsT(f32(inputs["L2_attn_w_qkv"]), 256)
    d["l2a_Wo"] = tile_lhsT(f32(inputs["L2_attn_w_out"]), 512)
    d["l2a_bo"] = row_bias(inputs["L2_attn_b_out"])
    d["l2a_proj"] = f32(inputs["L2_attn_proj"]).reshape(8, 128, 256)

    d["l3a_Wdq"] = tile_lhsT(f32(inputs["L3_attn_Wd_q_w"]), 512)
    d["l3a_Wdkv"] = tile_lhsT(f32(inputs["L3_attn_Wd_kv_w"]), 512)
    d["l3a_bdq"] = chunk_bias(inputs["L3_attn_Wd_q_b"])
    d["l3a_bdkv"] = chunk_bias(inputs["L3_attn_Wd_kv_b"])
    d["l3a_Wuq"] = tile_lhsT(f32(inputs["L3_attn_Wu_q_w"]), 1024)
    d["l3a_Wqr"] = tile_lhsT(f32(inputs["L3_attn_Wq_r_w"]), 1024)
    d["l3a_Wkr"] = tile_lhsT(f32(inputs["L3_attn_Wk_r_w"]), 1024)
    d["l3a_Wuk"] = tile_lhsT(f32(inputs["L3_attn_Wu_k_w"]), 1024)
    d["l3a_Wuv"] = tile_lhsT(f32(inputs["L3_attn_Wu_v_w"]), 1024)
    brow = np.concatenate([f32(inputs["L3_attn_Wu_q_b"]),
                           f32(inputs["L3_attn_Wq_r_b"]),
                           f32(inputs["L3_attn_Wk_r_b"]),
                           f32(inputs["L3_attn_Wu_k_b"]),
                           f32(inputs["L3_attn_Wu_v_b"])]).reshape(1, 5120)
    d["l3a_brow"] = np.ascontiguousarray(brow)
    d["l3a_Wo"] = tile_lhsT(f32(inputs["L3_attn_Wo_w"]), 512)
    d["l3a_bo"] = row_bias(inputs["L3_attn_Wo_b"])
    pos = np.arange(T, dtype=np.float64)[:, None]
    ids = np.arange(512, dtype=np.float64)
    theta = (10000.0 ** (-2.0 * ids)) / 512.0
    r = pos * theta[None, :]
    d["c_cos"] = np.cos(r).astype(np.float32)
    d["c_sin"] = np.sin(r).astype(np.float32)

    d["l3f_wfc"] = f32(inputs["L3_ffn_w_fc"]).reshape(32, 128, 1024)
    d["l3f_wproj"] = f32(inputs["L3_ffn_w_proj"]).reshape(8, 128, 4096)
    d["l3f_vT"] = tile_lhsT(f32(inputs["L3_ffn_v"]).T, 16)
    d["l3f_fbT"] = np.ascontiguousarray(f32(inputs["L3_ffn_fast_b"]).T)
    d["l3f_uT"] = np.ascontiguousarray(f32(inputs["L3_ffn_u"]).T)
    d["l3f_gate"] = f32(inputs["L3_ffn_gate"]).reshape(1, 1)
    return d


def prep_x(x_core):
    return np.ascontiguousarray(np.asarray(x_core, dtype=np.float32).T)



class Ctx:
    def __init__(self, nc, T, stack):
        self.nc = nc
        self.T = T
        self.NT = T // 512
        self.NTT = T // 128
        self.stack = stack
        self.sems = {}
        self._wait_hw = {}
        self.dram = {}
        self.input_names = []
        self.pe = nc.tensor
        self.dve = nc.vector
        self.act = nc.scalar
        self.pool = nc.gpsimd
        self.syncq = nc.sync

    def mksem(self, *names):
        for n in names:
            self.sems[n] = [self.stack.enter_context(self.nc.semaphore(n)), 0]

    def inc(self, name, instr, k=1):
        h = self.sems[name]
        instr.then_inc(h[0], k)
        h[1] += k
        return h[1]

    def val(self, name):
        return self.sems[name][1]

    def wait(self, eng, name, v=None):
        h = self.sems[name]
        v = h[1] if v is None else v
        if v <= 0:
            return
        key = id(eng)
        hw = self._wait_hw.setdefault(key, {})
        if hw.get(name, -1) >= v:
            return
        hw[name] = v
        eng.wait_ge(h[0], v)

    def din(self, name, shape, dtype=F32):
        ap = self.nc.dram_tensor(name, list(shape), dtype, kind="ExternalInput").ap()
        self.dram[name] = ap
        self.input_names.append(name)
        return ap

    def dout(self, name, shape, dtype=F32):
        ap = self.nc.dram_tensor(name, list(shape), dtype, kind="ExternalOutput").ap()
        self.dram[name] = ap
        return ap

    def dtmp(self, name, shape, dtype=F32):
        ap = self.nc.dram_tensor(name, list(shape), dtype, kind="Internal").ap()
        self.dram[name] = ap
        return ap

    def sb(self, name, shape, dtype, stack=None):
        return (stack or self.stack).enter_context(
            self.nc.sbuf_tensor(name, list(shape), dtype))

    def ps(self, name, shape=(128, 512), dtype=F32, stack=None):
        return (stack or self.stack).enter_context(
            self.nc.psum_tensor(name, list(shape), dtype))

    def load_cast(self, dst_ap, src_ap):
        if self.val("dmaw"):
            self.wait(self.pool, "dmaw")
        ins = self.pool.dma_start(out=dst_ap, in_=src_ap)
        return self.inc("dmaw", ins, 16)

    def load_plain(self, dst_ap, src_ap):
        if self.val("dmas"):
            self.wait(self.syncq, "dmas")
        ins = self.syncq.dma_start(out=dst_ap, in_=src_ap)
        return self.inc("dmas", ins, 16)

    def store(self, dst_ap, src_ap):
        if self.val("dmao"):
            self.wait(self.syncq, "dmao")
        ins = self.syncq.dma_start(out=dst_ap, in_=src_ap)
        return self.inc("dmao", ins, 16)

    def phase_barrier(self):
        for eng in (self.pe, self.dve, self.act, self.pool, self.syncq):
            for s in ("pe", "act", "dve", "pool", "dmaw", "dmas", "dmao"):
                if self.val(s):
                    self.wait(eng, s)


class PsRot:
    def __init__(self, c, tiles):
        self.c = c
        self.tiles = tiles
        self.user = [None] * len(tiles)
        self.i = -1

    def acquire(self):
        self.i = (self.i + 1) % len(self.tiles)
        if self.user[self.i] is not None:
            self.c.wait(self.c.pe, *self.user[self.i])
        return self.tiles[self.i]

    def release(self, semname, val):
        self.user[self.i] = (semname, val)


def setup_consts(c):
    k = {}
    c.din("c_ident", (128, 128))
    c.din("c_ones_col", (128, 1))
    c.din("c_ones_row", (1, 512))
    c.din("c_ones_row_f32", (1, 128))
    nkt = c.T // 128
    c.din("c_alibi", (128, 16 * nkt))
    k["ident"] = c.sb("k_ident", (128, 128), BF16)
    k["ident_f32"] = c.sb("k_ident_f32", (128, 128), F32)
    k["ones_col"] = c.sb("k_ones_col", (128, 1), BF16)
    k["ones_row"] = c.sb("k_ones_row", (1, 512), BF16)
    k["ones_row_f32"] = c.sb("k_ones_row_f32", (1, 128), F32)
    k["alibi"] = c.sb("k_alibi", (128, 16 * nkt), F32)
    c.load_cast(k["ident"][:], c.dram["c_ident"][:])
    c.load_plain(k["ident_f32"][:], c.dram["c_ident"][:])
    c.load_cast(k["ones_col"][:], c.dram["c_ones_col"][:])
    c.load_cast(k["ones_row"][:], c.dram["c_ones_row"][:])
    c.load_plain(k["ones_row_f32"][:], c.dram["c_ones_row_f32"][:])
    c.load_plain(k["alibi"][:], c.dram["c_alibi"][:])
    return k


def emit_load_stream(c, ST, src_name, src_dtype):
    src = c.dram[src_name]
    for cc in range(8):
        sl = src[cc * 128:(cc + 1) * 128, :]
        if src_dtype == F32:
            c.load_cast(ST[cc][:], sl)
        else:
            c.load_plain(ST[cc][:], sl)


def emit_store_stream_mid(c, ST, dst_name):
    dst = c.dram[dst_name]
    for cc in range(8):
        c.store(dst[cc * 128:(cc + 1) * 128, :], ST[cc][:])


def mm_chunks(c, ps_ap, slab, m_w, j0, width, SRC, tsl, nchunks=8, bias_lhsT=None,
              bias_rhs=None):
    for cc in range(nchunks):
        last = (cc == nchunks - 1) and bias_lhsT is None
        ins = c.pe.matmul(ps_ap, slab[:, cc * m_w + j0: cc * m_w + j0 + width],
                          SRC[cc][:, tsl], start=(cc == 0), stop=last)
    if bias_lhsT is not None:
        ins = c.pe.matmul(ps_ap, bias_lhsT, bias_rhs, start=False, stop=True)
    return c.inc("pe", ins)



class LnScratch:
    def __init__(self, c, tag, stack, alloc_r=True):
        self.R = ([c.sb(f"{tag}_R{i}", (128, 512), F32, stack) for i in range(8)]
                  if alloc_r else None)
        self.Rb = [c.sb(f"{tag}_Rb{i}", (128, 512), BF16, stack) for i in range(8)]
        self.Sq = [c.sb(f"{tag}_Sq{i}", (128, 512), BF16, stack) for i in range(8)]
        self.row = c.sb(f"{tag}_row", (1, 2048), F32, stack)
        self.ps_s = c.ps(f"{tag}_ps_s", stack=stack)
        self.ps_q = c.ps(f"{tag}_ps_q", stack=stack)
        self.ps_b = c.ps(f"{tag}_ps_b", stack=stack)
        self.ps_b2 = c.ps(f"{tag}_ps_b2", stack=stack)


def emit_ln_tq(c, k, ln, ST, tq, o_kind, o_src, final_out=None, o_scale=None,
               R_tiles=None):
    ts = slice(tq * 512, tq * 512 + 512)
    R = R_tiles if R_tiles is not None else [t_[:] for t_ in ln.R]
    for oc in range(8):
        if o_kind != "R_ready":
            osrc = o_src[oc][:] if o_kind == "psum" else o_src[oc][:, ts]
            if o_scale is None:
                c.inc("dve", c.dve.tensor_tensor(R[oc], osrc, ST[oc][:, ts], ALU.add))
            else:
                c.inc("dve", c.dve.scalar_tensor_tensor(R[oc], osrc, o_scale,
                                                        ST[oc][:, ts], ALU.mult, ALU.add))
        c.wait(c.dve, "dve")
        c.inc("dve", c.dve.tensor_copy(ln.Rb[oc][:], R[oc]))
        c.wait(c.act, "dve")
        c.inc("act", c.act.activation(out=ln.Sq[oc][:], in_=R[oc], func=AF.Square))
    c.wait(c.pe, "dve")
    c.wait(c.pe, "act")
    for oc in range(8):
        c.pe.matmul(ln.ps_s[0:1, :], k["ones_col"][:], ln.Rb[oc][:],
                    start=(oc == 0), stop=(oc == 7))
    for oc in range(8):
        ins = c.pe.matmul(ln.ps_q[0:1, :], k["ones_col"][:], ln.Sq[oc][:],
                          start=(oc == 0), stop=(oc == 7))
    pe_stats = c.inc("pe", ins)
    rw = ln.row
    m_r, v_r, rs_r, mr_r = rw[:, 0:512], rw[:, 512:1024], rw[:, 1024:1536], rw[:, 1536:2048]
    c.wait(c.dve, "pe", pe_stats)
    c.inc("dve", c.dve.tensor_scalar(m_r, ln.ps_s[0:1, :], 1.0 / 1024.0, None, ALU.mult))
    c.inc("dve", c.dve.tensor_scalar(v_r, ln.ps_q[0:1, :], 1.0 / 1024.0, 1e-5, ALU.mult,
                                     ALU.add))
    c.wait(c.dve, "dve")
    c.inc("dve", c.dve.tensor_tensor(mr_r, m_r, m_r, ALU.mult))
    c.wait(c.dve, "dve")
    dve_v = c.inc("dve", c.dve.tensor_tensor(v_r, v_r, mr_r, ALU.subtract))
    c.wait(c.act, "dve", dve_v)
    c.wait(c.act, "act")
    c.inc("act", c.act.activation(out=rs_r, in_=v_r, func=AF.Ln))
    c.wait(c.act, "act")
    act_rs = c.inc("act", c.act.activation(out=rs_r, in_=rs_r, func=AF.Exp, scale=-0.5))
    c.wait(c.dve, "act", act_rs)
    dve_mr = c.inc("dve", c.dve.tensor_tensor(mr_r, m_r, rs_r, ALU.mult))
    c.wait(c.pe, "dve", dve_mr)
    c.pe.matmul(ln.ps_b[:], k["ones_row_f32"][:], rs_r, start=True, stop=True)
    pe_bc = c.inc("pe", c.pe.matmul(ln.ps_b2[:], k["ones_row_f32"][:], mr_r,
                                    start=True, stop=True))
    c.wait(c.dve, "pe", pe_bc)
    for oc in range(8):
        c.inc("dve", c.dve.tensor_tensor(R[oc], R[oc], ln.ps_b[:], ALU.mult))
    c.wait(c.dve, "dve")
    for oc in range(8):
        if final_out is None:
            c.inc("dve", c.dve.tensor_tensor(ST[oc][:, ts], R[oc], ln.ps_b2[:],
                                             ALU.subtract))
        else:
            dram_ap, stage, stage_user = final_out
            if stage_user[oc] is not None:
                c.wait(c.dve, "dmao", stage_user[oc])
            dv = c.inc("dve", c.dve.tensor_tensor(stage[oc][:], R[oc], ln.ps_b2[:],
                                                  ALU.subtract))
            c.wait(c.syncq, "dve", dv)
            stage_user[oc] = c.store(dram_ap[oc * 128:(oc + 1) * 128, ts], stage[oc][:])
    return c.val("dve")



class AttnScratch:
    def __init__(self, c, tag, stack, extra_s=()):
        self.ps_s = [c.ps(f"{tag}_ps_s{i}", stack=stack) for i in range(3)]
        self.ps_s += list(extra_s)
        self.ps_o = [c.ps(f"{tag}_ps_o{i}", stack=stack) for i in range(2)]
        self.ps_bc = c.ps(f"{tag}_ps_bc", stack=stack)
        self.P = [c.sb(f"{tag}_P{i}", (128, 512), BF16, stack)
                  for i in range(len(self.ps_s))]
        self.yraw = c.sb(f"{tag}_yraw", (64, 512), BF16, stack)
        self.prow = c.sb(f"{tag}_prow", (1, 1024), F32, stack)
        self.po_user = [None, None]
        self.bc_user = 0
        self.bc_pe = 0


def _attn_normalize(c, k, a, po, bank, yslice, avv):
    yr, pr = a.yraw, a.prow
    c.wait(c.dve, "pe", avv)
    c.wait(c.dve, "dve")
    c.inc("dve", c.dve.tensor_copy(yr[:], po[0:64, :]))
    c.wait(c.act, "pe", avv)
    c.wait(c.act, "pe", a.bc_pe)
    c.wait(c.act, "act")
    c.inc("act", c.act.activation(out=pr[:, 0:512], in_=po[64:65, :], func=AF.Ln))
    c.wait(c.act, "act")
    ar = c.inc("act", c.act.activation(out=pr[:, 512:1024], in_=pr[:, 0:512],
                                       func=AF.Exp, scale=-1.0))
    c.wait(c.pe, "act", ar)
    c.wait(c.pe, "dve", a.bc_user)
    bc = c.inc("pe", c.pe.matmul(a.ps_bc[0:64, :], k["ones_row_f32"][:, 0:64],
                                 pr[:, 512:1024], start=True, stop=True))
    a.bc_pe = bc
    c.wait(c.dve, "pe", bc)
    c.wait(c.dve, "dve")
    ydone = c.inc("dve", c.dve.tensor_tensor(yslice, yr[:], a.ps_bc[0:64, :], ALU.mult))
    a.bc_user = ydone
    a.po_user[bank] = (ydone, ar)


def emit_attn_group(c, k, a, g, QT, KT, vt_ap, YT, exp_args, causal):
    T, NT = c.T, c.NT
    nkt_full = T // 128
    for hp in range(2):
        hh0, hh1 = 2 * hp, 2 * hp + 1
        ktile, qtile = KT[hp], QT[hp]
        for tq in range(NT):
            nkt = min(nkt_full, (tq + 1) * 4) if causal else nkt_full
            for b in range(2):
                if a.po_user[b] is not None:
                    dv, av = a.po_user[b]
                    c.wait(c.pe, "dve", dv)
                    c.wait(c.pe, "act", av)
            NS = len(a.ps_s)
            exp_done = []
            av_done = []
            for kt in range(nkt):
                for hx, hh in ((0, hh0), (1, hh1)):
                    n = 2 * kt + hx
                    sb_i = n % NS
                    pb = hx * 64
                    if n >= NS:
                        c.wait(c.pe, "act", exp_done[n - NS])
                    ins = c.pe.matmul(a.ps_s[sb_i][:],
                                      ktile[pb:pb + 64, kt * 128:(kt + 1) * 128],
                                      qtile[pb:pb + 64, tq * 512:(tq + 1) * 512],
                                      start=True, stop=True)
                    qk = c.inc("pe", ins)
                masked = causal and (kt * 128 + 127 > tq * 512)
                for hx, hh in ((0, hh0), (1, hh1)):
                    n = 2 * kt + hx
                    sb_i = n % NS
                    habs = g * 4 + hh
                    sc, bias = exp_args(habs, kt)
                    c.wait(c.act, "pe", qk)
                    if n >= NS:
                        c.wait(c.act, "pe", av_done[n - NS])
                    ins = c.act.activation(out=a.P[sb_i][:], in_=a.ps_s[sb_i][:],
                                           func=AF.Exp, scale=sc, bias=bias)
                    ed = c.inc("act", ins)
                    exp_done.append(ed)
                    if masked:
                        c.wait(c.pool, "act", ed)
                        ins = c.pool.affine_select(
                            out=a.P[sb_i][:], in_=a.P[sb_i][:],
                            pattern=[[1, 512]], compare_op=ALU.is_ge, fill=0.0,
                            base=tq * 512 - kt * 128, channel_multiplier=-1)
                        pd = c.inc("pool", ins)
                if masked:
                    c.wait(c.pe, "pool", pd)
                else:
                    c.wait(c.pe, "act", ed)
                for hx, hh in ((0, hh0), (1, hh1)):
                    n = 2 * kt + hx
                    sb_i = n % NS
                    ins = c.pe.matmul(a.ps_o[hx][0:65, :], vt_ap(kt, hh),
                                      a.P[sb_i][:], start=(kt == 0),
                                      stop=(kt == nkt - 1))
                    av_done.append(c.inc("pe", ins))
                if masked:
                    exp_done[-1] = c.val("act")
                    exp_done[-2] = c.val("act")
            for hx, hh in ((0, hh0), (1, hh1)):
                habs = g * 4 + hh
                yslice = YT[habs // 2][(habs % 2) * 64:(habs % 2) * 64 + 64,
                                       tq * 512:(tq + 1) * 512]
                _attn_normalize(c, k, a, a.ps_o[hx], hx, yslice, av_done[-2 + hx])



def emit_std_attn(c, k, ST, outer, cfg):
    T, NT, NTT = c.T, c.NT, c.NTT
    tag = cfg["tag"]
    YT = [c.sb(f"{tag}_yt{i}", (128, T), BF16, outer) for i in range(8)]
    bo_r = c.sb(f"{tag}_bo", (1, 1024), BF16, outer) if cfg.get("out_bias") else None

    with ExitStack() as phA:
        QT = [c.sb(f"{tag}_qt{i}", (128, T), BF16, phA) for i in range(2)]
        KT = [c.sb(f"{tag}_kt{i}", (128, T), BF16, phA) for i in range(2)]
        VT = [c.sb(f"{tag}_vt{i}", (128, 260), BF16, phA) for i in range(NTT)]
        wq = c.sb(f"{tag}_wq", (128, 8 * 256), BF16, phA)
        wk = c.sb(f"{tag}_wk", (128, 8 * 256), BF16, phA)
        wv = c.sb(f"{tag}_wv", (128, 8 * 256), BF16, phA)
        if cfg.get("qk_bias"):
            bq_t = c.sb(f"{tag}_bq", (128, 8), F32, phA)
            bk_t = c.sb(f"{tag}_bk", (128, 8), F32, phA)
        if cfg.get("v_bias"):
            bv_r = c.sb(f"{tag}_bv", (1, 1024), BF16, phA)
        rot = PsRot(c, [c.ps(f"{tag}_psr{i}", stack=phA) for i in range(2)])
        a = AttnScratch(c, f"{tag}at", phA, extra_s=[t for t in rot.tiles])

        if cfg.get("qk_bias"):
            c.load_plain(bq_t[:], c.dram[cfg["qk_bias"][0]][:])
            c.load_plain(bk_t[:], c.dram[cfg["qk_bias"][1]][:])
        if cfg.get("v_bias"):
            c.load_cast(bv_r[:], c.dram[cfg["v_bias"]][:])
        if cfg.get("out_bias"):
            c.load_cast(bo_r[:], c.dram[cfg["out_bias"]][:])
        bias_w = c.val("dmaw")
        bias_s = c.val("dmas")

        for tt in range(NTT):
            v3 = VT[tt].ap().rearrange("p (h y) -> p h y", y=65)
            c.inc("dve", c.dve.memset(v3[:, :, 64:65], 1.0))
        vones = c.val("dve")

        qkv_pe_done = 0
        for g in range(4):
            c.wait(c.pool, "pe", qkv_pe_done)
            wq_src, wk_src, wv_src = cfg["w_group"](c, g)
            c.load_cast(wq[:], wq_src)
            c.load_cast(wk[:], wk_src)
            c.load_cast(wv[:], wv_src)
            wl = c.val("dmaw")

            c.wait(c.pe, "dmaw", wl)
            c.wait(c.pe, "dmas", bias_s)
            c.wait(c.act, "dmas", bias_s)
            c.wait(c.act, "dmaw", bias_w)
            for qi, (dst, slab) in enumerate(((QT, wq), (KT, wk))):
                for fc in range(2):
                    for tq in range(NT):
                        pstile = rot.acquire()
                        pv = mm_chunks(c, pstile[:], slab, 256, fc * 128, 128,
                                       ST, slice(tq * 512, (tq + 1) * 512))
                        c.wait(c.act, "pe", pv)
                        gc_ = g * 2 + fc
                        if cfg.get("qk_bias"):
                            bias_t = bq_t if qi == 0 else bk_t
                            ins = c.act.activation(
                                out=dst[fc][:, tq * 512:(tq + 1) * 512],
                                in_=pstile[:], func=AF.Identity,
                                bias=bias_t[:, gc_:gc_ + 1])
                        else:
                            ins = c.act.activation(
                                out=dst[fc][:, tq * 512:(tq + 1) * 512],
                                in_=pstile[:], func=AF.Identity)
                        rot.release("act", c.inc("act", ins))
            c.wait(c.pe, "dve", vones)
            for tt in range(NTT):
                pstile = rot.acquire()
                nstop = not cfg.get("v_bias")
                for cc in range(8):
                    ins = c.pe.matmul(pstile[:, 0:256],
                                      ST[cc][:, tt * 128:(tt + 1) * 128],
                                      wv[:, cc * 256:(cc + 1) * 256],
                                      start=(cc == 0), stop=(nstop and cc == 7))
                if cfg.get("v_bias"):
                    ins = c.pe.matmul(pstile[:, 0:256], k["ones_row"][0:1, 0:128],
                                      bv_r[0:1, g * 256:(g + 1) * 256],
                                      start=False, stop=True)
                pv = c.inc("pe", ins)
                c.wait(c.dve, "pe", pv)
                c.wait(c.dve, "dve")
                src3 = pstile.ap()[:, 0:256].rearrange("p (h y) -> p h y", y=64)
                dst3 = VT[tt].ap().rearrange("p (h y) -> p h y", y=65)[:, :, 0:64]
                rot.release("dve", c.inc("dve", c.dve.tensor_copy(dst3, src3)))
            qkv_pe_done = c.val("pe")
            c.wait(c.pe, "act")
            c.wait(c.pe, "dve")
            emit_attn_group(c, k, a, g, QT, KT,
                            lambda kt, hh: VT[kt][:, hh * 65:hh * 65 + 65],
                            YT, cfg["exp_args"], cfg["causal"])
            rot.user[0] = rot.user[1] = ("act", c.val("act"))

    c.phase_barrier()
    with ExitStack() as phB:
        if cfg["out"][0] == "wo":
            emit_out_proj_ln(c, k, ST, YT, cfg["out"][1], bo_r, tag, phB)
        else:
            ln = LnScratch(c, f"{tag}ln", phB)
            c.wait(c.dve, "pe")
            c.wait(c.dve, "dve")
            for tq in range(NT):
                emit_ln_tq(c, k, ln, ST, tq, "sbuf", YT)
    return


def emit_out_proj_ln(c, k, ST, YT, wo_name, bo_r, tag, phB):
    NT = c.NT
    ln = LnScratch(c, f"{tag}ln", phB)
    wo = [c.sb(f"{tag}_wo{i}", (128, 8 * 512), BF16, phB) for i in range(2)]
    orot = PsRot(c, [c.ps(f"{tag}_psoc{i}", stack=phB) for i in range(4)])
    wov = [c.load_cast(wo[0][:], c.dram[wo_name][0, :, :]),
           c.load_cast(wo[1][:], c.dram[wo_name][1, :, :])]
    c.wait(c.pe, "dve")
    c.wait(c.pe, "act")
    for tq in range(NT):
        ts = slice(tq * 512, (tq + 1) * 512)
        for oc in range(8):
            pstile = orot.acquire()
            c.wait(c.pe, "dmaw", wov[oc // 4])
            pv = mm_chunks(c, pstile[:], wo[oc // 4], 512, (oc % 4) * 128, 128,
                           YT, ts,
                           bias_lhsT=bo_r[0:1, oc * 128:(oc + 1) * 128],
                           bias_rhs=k["ones_row"][0:1, 0:512])
            c.wait(c.dve, "pe", pv)
            c.wait(c.dve, "dve")
            dv = c.inc("dve", c.dve.tensor_tensor(ln.R[oc][:], pstile[:],
                                                  ST[oc][:, ts], ALU.add))
            orot.release("dve", dv)
        emit_ln_tq(c, k, ln, ST, tq, "R_ready", None)


def declare_l0_attn(c):
    c.din("l0a_Wq", (4, 128, 8 * 256))
    c.din("l0a_Wk", (4, 128, 8 * 256))
    c.din("l0a_Wv", (4, 128, 8 * 256))
    c.din("l0a_Wo", (2, 128, 8 * 512))
    c.din("l0a_bq", (128, 8))
    c.din("l0a_bk", (128, 8))
    c.din("l0a_bv", (1, 1024))
    c.din("l0a_bo", (1, 1024))


def emit_l0_attn(c, k, ST, outer):
    cfg = {
        "tag": "l0",
        "causal": True,
        "exp_args": lambda habs, kt: (1.0 / 8.0, 0.0),
        "w_group": lambda c_, g: (c_.dram["l0a_Wq"][g, :, :],
                                  c_.dram["l0a_Wk"][g, :, :],
                                  c_.dram["l0a_Wv"][g, :, :]),
        "qk_bias": ("l0a_bq", "l0a_bk"),
        "v_bias": "l0a_bv",
        "out": ("wo", "l0a_Wo"),
        "out_bias": "l0a_bo",
    }
    emit_std_attn(c, k, ST, outer, cfg)


def declare_l1_attn(c):
    c.din("l1a_W", (12, 128, 8 * 256))


def emit_l1_attn(c, k, ST, outer):
    nkt = c.T // 128
    cfg = {
        "tag": "l1",
        "causal": False,
        "exp_args": lambda habs, kt: (1.0 / 32.0,
                                      k["alibi"][:, habs * nkt + kt:habs * nkt + kt + 1]),
        "w_group": lambda c_, g: (c_.dram["l1a_W"][4 + g, :, :],
                                  c_.dram["l1a_W"][g, :, :],
                                  c_.dram["l1a_W"][8 + g, :, :]),
        "out": ("direct",),
    }
    emit_std_attn(c, k, ST, outer, cfg)



def declare_l0_ffn(c):
    c.din("l0f_w1", (2, 128, 8 * 512))
    c.din("l0f_w2", (1, 128, 8 * 512))
    c.din("l0f_w3", (1, 128, 4 * 256))
    c.din("l0f_w4r", (2, 128, 128))
    c.din("l0f_b1", (128, 8))
    c.din("l0f_b2", (128, 4))
    c.din("l0f_b3", (128, 2))
    c.din("l0f_b4r", (128, 1))
    c.din("l0f_gw1", (1, 128, 8 * 512))
    c.din("l0f_gw2", (1, 128, 4 * 256))
    c.din("l0f_gw3", (128, 2))
    c.din("l0f_gb1", (128, 4))
    c.din("l0f_gb2", (128, 2))
    c.din("l0f_gb3", (1, 1))
    c.din("l0f_ar", (1, 1))


def _gelu_chain(c, z_sb, e_sb, t_sb, out_ap, ps_ap, bias_ap, scale=1.0):
    c.wait(c.act, "pe")
    c.wait(c.act, "act")
    c.wait(c.act, "dve")
    c.inc("act", c.act.activation(out=z_sb, in_=ps_ap, func=AF.Identity,
                                  bias=bias_ap, scale=scale))
    c.wait(c.act, "act")
    av = c.inc("act", c.act.activation(out=e_sb, in_=z_sb, func=AF.Erf,
                                       scale=0.7071067811865476))
    c.wait(c.dve, "act", av)
    c.wait(c.dve, "dve")
    c.inc("dve", c.dve.tensor_scalar(t_sb, e_sb, 0.5, 0.5, ALU.mult, ALU.add))
    c.wait(c.dve, "dve")
    return c.inc("dve", c.dve.tensor_tensor(out_ap, t_sb, z_sb, ALU.mult))


def emit_l0_ffn(c, k, ST, outer):
    T, NT = c.T, c.NT
    BIG = [c.sb(f"l0f_big{i}", (128, T), BF16, outer) for i in range(8)]

    with ExitStack() as ph:
        w1 = [c.sb(f"l0f_w1_{i}", (128, 8 * 512), BF16, ph) for i in range(2)]
        w2 = c.sb("l0f_w2t", (128, 8 * 512), BF16, ph)
        w3 = c.sb("l0f_w3t", (128, 4 * 256), BF16, ph)
        w4r = [c.sb(f"l0f_w4r{i}", (128, 128), BF16, ph) for i in range(2)]
        b1 = c.sb("l0f_b1t", (128, 8), F32, ph)
        b2 = c.sb("l0f_b2t", (128, 4), F32, ph)
        b3 = c.sb("l0f_b3t", (128, 2), F32, ph)
        b4r = c.sb("l0f_b4rt", (128, 1), F32, ph)
        AB = c.sb("l0f_AB", (128, 2), F32, ph)
        h1b = [c.sb(f"l0f_h1b{i}", (128, 512), BF16, ph) for i in range(8)]
        h2b = [c.sb(f"l0f_h2b{i}", (128, 512), BF16, ph) for i in range(4)]
        h3b = [c.sb(f"l0f_h3b{i}", (128, 512), BF16, ph) for i in range(2)]
        zsb = c.sb("l0f_z", (128, 512), F32, ph)
        esb = c.sb("l0f_e", (128, 512), F32, ph)
        tsb = c.sb("l0f_t", (128, 512), F32, ph)
        cp = c.sb("l0f_cp", (128, 512), F32, ph)
        u_t = c.sb("l0f_u", (128, 512), F32, ph)
        tmpb = c.sb("l0f_tmpb", (128, 512), BF16, ph)
        grow = c.sb("l0f_grow", (1, 4), F32, ph)
        gcs = c.sb("l0f_gcs", (128, 8), F32, ph)
        gcb = c.sb("l0f_gcb", (128, 8), BF16, ph)
        g1 = c.sb("l0f_g1", (128, 5), F32, ph)
        g1b = c.sb("l0f_g1b", (128, 4), BF16, ph)
        g2b = c.sb("l0f_g2b", (128, 2), BF16, ph)
        gw1 = c.sb("l0f_gw1t", (128, 8 * 512), BF16, ph)
        gw2 = c.sb("l0f_gw2t", (128, 4 * 256), BF16, ph)
        gw3 = c.sb("l0f_gw3t", (128, 2), BF16, ph)
        gb1 = c.sb("l0f_gb1t", (128, 4), F32, ph)
        gb2 = c.sb("l0f_gb2t", (128, 2), F32, ph)
        gb3 = c.sb("l0f_gb3t", (1, 1), F32, ph)
        ar_t = c.sb("l0f_art", (1, 1), F32, ph)
        rot = PsRot(c, [c.ps(f"l0f_psr{i}", stack=ph) for i in range(4)])

        c.load_cast(w1[0][:], c.dram["l0f_w1"][0])
        c.load_cast(w1[1][:], c.dram["l0f_w1"][1])
        c.load_cast(w2[:], c.dram["l0f_w2"][0])
        c.load_cast(w3[:], c.dram["l0f_w3"][0])
        c.load_cast(w4r[0][:], c.dram["l0f_w4r"][0])
        c.load_cast(w4r[1][:], c.dram["l0f_w4r"][1])
        c.load_cast(gw1[:], c.dram["l0f_gw1"][0])
        c.load_cast(gw2[:], c.dram["l0f_gw2"][0])
        c.load_cast(gw3[:], c.dram["l0f_gw3"][:])
        wl = c.val("dmaw")
        for nm, t_ in (("l0f_b1", b1), ("l0f_b2", b2), ("l0f_b3", b3), ("l0f_b4r", b4r),
                       ("l0f_gb1", gb1), ("l0f_gb2", gb2), ("l0f_gb3", gb3),
                       ("l0f_ar", ar_t)):
            c.load_plain(t_[:], c.dram[nm][:])
        bl = c.val("dmas")

        c.wait(c.dve, "dve")
        for cc in range(8):
            c.inc("dve", c.dve.tensor_copy(BIG[cc][:], ST[cc][:]))
        big_ready = c.val("dve")

        c.wait(c.dve, "dve")
        for cc in range(8):
            c.inc("dve", c.dve.tensor_reduce(gcs[:, cc:cc + 1], ST[cc][:],
                                             axis=AX.X, op=ALU.add))
        c.wait(c.dve, "dve")
        gcv = c.inc("dve", c.dve.tensor_copy(gcb[:], gcs[:]))
        c.wait(c.pe, "dve", gcv)
        c.wait(c.pe, "dmaw", wl)
        c.wait(c.pe, "dmas", bl)
        c.wait(c.act, "dmas", bl)

        def small_gelu(ps_col, bias_t, ncol, out_b, scale, bias_kind="elem"):
            if bias_kind == "pp":
                c.wait(c.act, "pe")
                c.wait(c.act, "act")
                ins = c.act.activation(out=out_b, in_=ps_col, func=AF.Gelu,
                                       bias=bias_t, scale=scale)
                return ("act", c.inc("act", ins))
            c.wait(c.dve, "pe")
            c.wait(c.dve, "dve")
            c.inc("dve", c.dve.tensor_scalar(zsb[:, 0:ncol], ps_col, scale, None, ALU.mult))
            c.wait(c.dve, "dve")
            zv = c.inc("dve", c.dve.tensor_tensor(zsb[:, 0:ncol], zsb[:, 0:ncol],
                                                  bias_t, ALU.add))
            c.wait(c.act, "dve", zv)
            c.wait(c.act, "act")
            return ("act", c.inc("act", c.act.activation(out=out_b, in_=zsb[:, 0:ncol],
                                                         func=AF.Gelu)))

        pst = rot.acquire()
        for fc in range(4):
            for ccs in range(8):
                ins = c.pe.matmul(pst[:, fc:fc + 1],
                                  gw1[:, ccs * 512 + fc * 128: ccs * 512 + fc * 128 + 128],
                                  gcb[:, ccs:ccs + 1], start=(ccs == 0), stop=(ccs == 7))
        g1v = c.inc("pe", ins)
        c.wait(c.dve, "pe", g1v)
        hv = small_gelu(pst[:, 0:4], gb1[:], 4, g1b[:], 1.0 / T)
        rot.release(*hv)
        c.wait(c.pe, hv[0], hv[1])
        pst = rot.acquire()
        for fc in range(2):
            for ic in range(4):
                ins = c.pe.matmul(pst[:, fc:fc + 1],
                                  gw2[:, ic * 256 + fc * 128: ic * 256 + fc * 128 + 128],
                                  g1b[:, ic:ic + 1], start=(ic == 0), stop=(ic == 3))
        g2v = c.inc("pe", ins)
        c.wait(c.dve, "pe", g2v)
        hv = small_gelu(pst[:, 0:2], gb2[:], 2, g2b[:], 1.0)
        rot.release(*hv)
        c.wait(c.pe, hv[0], hv[1])
        pst = rot.acquire()
        for ic in range(2):
            ins = c.pe.matmul(pst[0:1, 0:1], gw3[:, ic:ic + 1], g2b[:, ic:ic + 1],
                              start=(ic == 0), stop=(ic == 1))
        g3v = c.inc("pe", ins)
        c.wait(c.act, "pe", g3v)
        c.wait(c.act, "act")
        gfv = c.inc("act", c.act.activation(out=grow[:, 0:1], in_=pst[0:1, 0:1],
                                            func=AF.Sigmoid, bias=gb3[:, 0:1]))
        rot.release("act", gfv)
        c.wait(c.dve, "act", gfv)
        c.wait(c.dve, "dve")
        c.inc("dve", c.dve.tensor_tensor(grow[:, 1:2], grow[:, 0:1], ar_t[:, 0:1], ALU.mult))
        c.wait(c.dve, "dve")
        c.inc("dve", c.dve.tensor_scalar(grow[:, 1:2], grow[:, 1:2], 1.4, 0.1,
                                         ALU.mult, ALU.add))
        c.wait(c.dve, "dve")
        c.inc("dve", c.dve.tensor_tensor(grow[:, 2:3], grow[:, 1:2], ar_t[:, 0:1],
                                         ALU.subtract))
        c.wait(c.dve, "dve")
        abv = c.inc("dve", c.dve.tensor_scalar(grow[:, 3:4], ar_t[:, 0:1], 0.6, None,
                                               ALU.mult))
        c.wait(c.pe, "dve", abv)
        pst = rot.acquire()
        bc = c.inc("pe", c.pe.matmul(pst[:, 0:2], k["ones_row_f32"][:], grow[:, 2:4],
                                     start=True, stop=True))
        c.wait(c.dve, "pe", bc)
        c.wait(c.dve, "dve")
        abe = c.inc("dve", c.dve.tensor_copy(AB[:], pst[:, 0:2]))
        rot.release("dve", abe)

        c.wait(c.pe, "dve", big_ready)
        pairs = [(0, 4, 1.0, -1.0), (1, 5, 1.0, -1.0), (2, 6, -1.0, 1.0),
                 (3, 7, -1.0, 1.0)]
        for cyc in range(3):
            for tq in range(NT):
                ts_ = slice(tq * 512, (tq + 1) * 512)
                c.wait(c.pe, "dve")
                for fc in range(8):
                    pstile = rot.acquire()
                    slab = w1[fc // 4]
                    pv = mm_chunks(c, pstile[:], slab, 512, (fc % 4) * 128, 128,
                                   BIG, ts_)
                    c.wait(c.dve, "pe", pv)
                    hv = small_gelu(pstile[:], b1[:, fc:fc + 1], 512, h1b[fc][:], 1.0,
                                    bias_kind="pp")
                    rot.release(*hv)
                c.wait(c.pe, "dve")
                c.wait(c.pe, "act")
                for fc in range(4):
                    pstile = rot.acquire()
                    pv = mm_chunks(c, pstile[:], w2, 512, fc * 128, 128,
                                   h1b, slice(0, 512))
                    c.wait(c.dve, "pe", pv)
                    hv = small_gelu(pstile[:], b2[:, fc:fc + 1], 512, h2b[fc][:], 1.0,
                                    bias_kind="pp")
                    rot.release(*hv)
                c.wait(c.pe, "dve")
                c.wait(c.pe, "act")
                for fc in range(2):
                    pstile = rot.acquire()
                    pv = mm_chunks(c, pstile[:], w3, 256, fc * 128, 128,
                                   h2b, slice(0, 512), nchunks=4)
                    c.wait(c.dve, "pe", pv)
                    hv = small_gelu(pstile[:], b3[:, fc:fc + 1], 512, h3b[fc][:], 1.0,
                                    bias_kind="pp")
                    rot.release(*hv)
                c.wait(c.pe, "dve")
                c.wait(c.pe, "act")
                pstile = rot.acquire()
                for kc in range(2):
                    ins = c.pe.matmul(pstile[:], w4r[kc][:], h3b[kc][:],
                                      start=(kc == 0), stop=(kc == 1))
                pv = c.inc("pe", ins)
                c.wait(c.act, "pe", pv)
                c.wait(c.act, "act")
                tfv = c.inc("act", c.act.activation(out=cp[:], in_=pstile[:],
                                                    func=AF.Sigmoid, bias=b4r[:, 0:1]))
                rot.release("act", tfv)
                c.wait(c.dve, "act", tfv)
                c.wait(c.dve, "dve")
                c.inc("dve", c.dve.tensor_scalar(cp[:], cp[:], AB[:, 1:2], None, ALU.mult))
                c.wait(c.dve, "dve")
                c.inc("dve", c.dve.tensor_scalar(cp[:], cp[:], AB[:, 0:1], None, ALU.add))
                for (pa, pb, sa, sb_) in pairs:
                    c.wait(c.dve, "dve")
                    c.inc("dve", c.dve.tensor_copy(tmpb[:], BIG[pa][:, ts_]))
                    c.wait(c.dve, "dve")
                    c.inc("dve", c.dve.tensor_tensor(u_t[:], cp[:], BIG[pb][:, ts_],
                                                     ALU.mult))
                    c.wait(c.dve, "dve")
                    c.inc("dve", c.dve.tensor_tensor(
                        BIG[pa][:, ts_], BIG[pa][:, ts_], u_t[:],
                        ALU.add if sa > 0 else ALU.subtract))
                    c.wait(c.dve, "dve")
                    c.inc("dve", c.dve.tensor_tensor(u_t[:], cp[:], tmpb[:], ALU.mult))
                    c.wait(c.dve, "dve")
                    c.inc("dve", c.dve.tensor_tensor(
                        BIG[pb][:, ts_], BIG[pb][:, ts_], u_t[:],
                        ALU.add if sb_ > 0 else ALU.subtract))
                c.wait(c.pe, "dve")

    c.phase_barrier()
    with ExitStack() as phB:
        ln = LnScratch(c, "l0fln", phB)
        c.wait(c.dve, "dve")
        for tq in range(NT):
            emit_ln_tq(c, k, ln, ST, tq, "sbuf", BIG)
    return



def declare_l3_attn(c):
    c.din("l3a_Wdq", (1, 128, 8 * 512))
    c.din("l3a_Wdkv", (1, 128, 8 * 512))
    c.din("l3a_bdq", (128, 4))
    c.din("l3a_bdkv", (128, 4))
    c.din("l3a_Wuq", (1, 128, 4 * 1024))
    c.din("l3a_Wqr", (1, 128, 4 * 1024))
    c.din("l3a_Wkr", (1, 128, 8 * 1024))
    c.din("l3a_Wuk", (1, 128, 4 * 1024))
    c.din("l3a_Wuv", (1, 128, 4 * 1024))
    c.din("l3a_brow", (1, 5 * 1024))
    c.din("l3a_Wo", (2, 128, 8 * 512))
    c.din("l3a_bo", (1, 1024))
    c.din("c_cos", (c.T, 512))
    c.din("c_sin", (c.T, 512))


def emit_l3_attn(c, k, ST, outer):
    T, NT, NTT = c.T, c.NT, c.NTT
    tag = "l3a"
    SC = 1.0 / np.sqrt(np.float32(16 + 512 + 512))
    ctq_d = c.dtmp(f"{tag}_ctqd", (512, T), BF16)
    ctkv_d = c.dtmp(f"{tag}_ctkvd", (512, T), BF16)
    qt_d = c.dtmp(f"{tag}_qtd", (1024, T), BF16)
    kt_d = c.dtmp(f"{tag}_ktd", (1024, T), BF16)
    vt_d = c.dtmp(f"{tag}_vtd", (NTT, 128, 1040), BF16)

    with ExitStack() as ph0:
        wdq = c.sb(f"{tag}_wdqt", (128, 8 * 512), BF16, ph0)
        wdkv = c.sb(f"{tag}_wdkvt", (128, 8 * 512), BF16, ph0)
        bdq = c.sb(f"{tag}_bdqt", (128, 4), F32, ph0)
        bdkv = c.sb(f"{tag}_bdkvt", (128, 4), F32, ph0)
        ctile = [c.sb(f"{tag}_ct{i}", (128, 512), BF16, ph0) for i in range(3)]
        rot = PsRot(c, [c.ps(f"{tag}_ps0_{i}", stack=ph0) for i in range(3)])
        wl = c.load_cast(wdq[:], c.dram["l3a_Wdq"][0])
        wl = c.load_cast(wdkv[:], c.dram["l3a_Wdkv"][0])
        bl = c.load_plain(bdq[:], c.dram["l3a_bdq"][:])
        bl = c.load_plain(bdkv[:], c.dram["l3a_bdkv"][:])
        c.wait(c.pe, "dmaw", wl)
        c.wait(c.act, "dmas", bl)
        ct_user = [None] * 3
        ci = 0
        for (slab, bias_t, dstd) in ((wdq, bdq, ctq_d), (wdkv, bdkv, ctkv_d)):
            for dc in range(4):
                for tq in range(NT):
                    pst = rot.acquire()
                    pv = mm_chunks(c, pst[:], slab, 512, dc * 128, 128,
                                   ST, slice(tq * 512, (tq + 1) * 512))
                    ct = ctile[ci % 3]
                    if ct_user[ci % 3] is not None:
                        c.wait(c.act, "dmao", ct_user[ci % 3])
                    c.wait(c.act, "pe", pv)
                    c.wait(c.act, "act")
                    av = c.inc("act", c.act.activation(out=ct[:], in_=pst[:],
                                                       func=AF.Identity,
                                                       bias=bias_t[:, dc:dc + 1]))
                    rot.release("act", av)
                    c.wait(c.syncq, "act", av)
                    ct_user[ci % 3] = c.store(
                        dstd[dc * 128:(dc + 1) * 128, tq * 512:(tq + 1) * 512], ct[:])
                    ci += 1
        c.wait(c.syncq, "dmao")
    c.phase_barrier()

    with ExitStack() as ph1:
        wuq = c.sb(f"{tag}_wuqt", (128, 4096), BF16, ph1)
        wqr = c.sb(f"{tag}_wqrt", (128, 4096), BF16, ph1)
        wkr = c.sb(f"{tag}_wkrt", (128, 8192), BF16, ph1)
        wuk = c.sb(f"{tag}_wukt", (128, 4096), BF16, ph1)
        wuv = c.sb(f"{tag}_wuvt", (128, 4096), BF16, ph1)
        brow = c.sb(f"{tag}_browt", (1, 5120), BF16, ph1)
        cosB = [c.sb(f"{tag}_cosb{i}", (128, 2048), F32, ph1) for i in range(2)]
        sinB = [c.sb(f"{tag}_sinb{i}", (128, 2048), F32, ph1) for i in range(2)]
        ctqB = [c.sb(f"{tag}_ctqB{i}", (128, 2048), BF16, ph1) for i in range(2)]
        ctkB = [c.sb(f"{tag}_ctkB{i}", (128, 2048), BF16, ph1) for i in range(2)]
        qc_tok = c.sb(f"{tag}_qct", (128, 1024), BF16, ph1)
        q_tok = c.sb(f"{tag}_qtk", (128, 1024), BF16, ph1)
        k_tok = c.sb(f"{tag}_ktk", (128, 1024), BF16, ph1)
        rsc = [c.sb(f"{tag}_rs{i}", (128, 256), F32, ph1) for i in range(3)]
        QS = [c.sb(f"{tag}_qs{i}", (128, 512), BF16, ph1) for i in range(8)]
        KS = [c.sb(f"{tag}_ks{i}", (128, 512), BF16, ph1) for i in range(8)]
        VS = c.sb(f"{tag}_vs", (128, 1040), BF16, ph1)
        rot = PsRot(c, [c.ps(f"{tag}_ps1_{i}", stack=ph1) for i in range(4)])
        pstr = c.ps(f"{tag}_pstr", (128, 128), BF16, stack=ph1)

        for nm, t_ in (("l3a_Wuq", wuq), ("l3a_Wqr", wqr), ("l3a_Wkr", wkr),
                       ("l3a_Wuk", wuk), ("l3a_Wuv", wuv), ("l3a_brow", brow)):
            wl = c.load_cast(t_[:], c.dram[nm][0] if t_ is not brow else c.dram[nm][:])
        c.wait(c.pe, "dmaw", wl)
        v3o = VS.ap().rearrange("p (h y) -> p h y", y=65)
        c.inc("dve", c.dve.memset(v3o[:, :, 64:65], 1.0))

        def rope_combine(ps_half, nh, base_tok, out_tok, pe_val):
            cosS = cosT[:, nh * 256:(nh + 1) * 256]
            sinS = sinT[:, nh * 256:(nh + 1) * 256]
            x0 = ps_half.ap()[:, 0:512:2]
            x1 = ps_half.ap()[:, 1:512:2]
            bt = base_tok.ap()[:, nh * 512:(nh + 1) * 512]
            ot = out_tok.ap()[:, nh * 512:(nh + 1) * 512]
            c.wait(c.dve, "pe", pe_val)
            c.wait(c.dve, "dve")
            c.inc("dve", c.dve.tensor_tensor(rsc[0][:], x0, cosS, ALU.mult))
            c.inc("dve", c.dve.tensor_tensor(rsc[1][:], x1, sinS, ALU.mult))
            c.wait(c.dve, "dve")
            c.inc("dve", c.dve.tensor_tensor(rsc[0][:], rsc[0][:], rsc[1][:],
                                             ALU.subtract))
            c.wait(c.dve, "dve")
            c.inc("dve", c.dve.tensor_tensor(ot[:, 0:512:2], rsc[0][:],
                                             bt[:, 0:512:2], ALU.add))
            c.wait(c.dve, "dve")
            c.inc("dve", c.dve.tensor_tensor(rsc[0][:], x1, cosS, ALU.mult))
            c.inc("dve", c.dve.tensor_tensor(rsc[1][:], x0, sinS, ALU.mult))
            c.wait(c.dve, "dve")
            c.inc("dve", c.dve.tensor_tensor(rsc[0][:], rsc[0][:], rsc[1][:], ALU.add))
            c.wait(c.dve, "dve")
            return c.inc("dve", c.dve.tensor_tensor(ot[:, 1:512:2], rsc[0][:],
                                                    bt[:, 1:512:2], ALU.add))

        st_user = {}
        grp_pe_user = [0, 0]
        grp_dve_user = [0, 0]
        grp_ld = [None, None]

        def load_group(tq):
            sl = tq % 2
            g0 = tq * 512
            c.wait(c.syncq, "dve", grp_dve_user[sl])
            cl = c.load_plain(
                cosB[sl].ap().rearrange("p (t f) -> p t f", f=512),
                c.dram["c_cos"].rearrange("(n p) f -> p n f", p=128)[:, 4 * tq:4 * tq + 4, :])
            cl = c.load_plain(
                sinB[sl].ap().rearrange("p (t f) -> p t f", f=512),
                c.dram["c_sin"].rearrange("(n p) f -> p n f", p=128)[:, 4 * tq:4 * tq + 4, :])
            c.wait(c.pool, "pe", grp_pe_user[sl])
            bq = c.load_cast(
                ctqB[sl].ap().rearrange("p (d t) -> p d t", t=512),
                ctq_d.rearrange("(d p) t -> p d t", p=128)[:, :, g0:g0 + 512])
            bk = c.load_cast(
                ctkB[sl].ap().rearrange("p (d t) -> p d t", t=512),
                ctkv_d.rearrange("(d p) t -> p d t", p=128)[:, :, g0:g0 + 512])
            grp_ld[sl] = (cl, bk)

        load_group(0)
        for tt in range(NTT):
            t0 = tt * 128
            tq_cur = tt // 4
            sl = tq_cur % 2
            if tt % 4 == 0:
                if tq_cur + 1 < NT:
                    load_group(tq_cur + 1)
                cl, bk = grp_ld[sl]
                c.wait(c.pe, "dmaw", bk)
                c.wait(c.pe, "dmas", cl)
            toff = (tt % 4) * 512
            cosT = cosB[sl][:, toff:toff + 512]
            sinT = sinB[sl][:, toff:toff + 512]
            ctq_t = lambda dc: ctqB[sl][:, dc * 512 + (tt % 4) * 128:
                                        dc * 512 + (tt % 4) * 128 + 128]
            ctk_t = lambda dc: ctkB[sl][:, dc * 512 + (tt % 4) * 128:
                                        dc * 512 + (tt % 4) * 128 + 128]
            for nh in range(2):
                pst = rot.acquire()
                for dc in range(4):
                    c.pe.matmul(pst[:], ctq_t(dc),
                                wuq[:, dc * 1024 + nh * 512: dc * 1024 + nh * 512 + 512],
                                start=(dc == 0), stop=False)
                pv = c.inc("pe", c.pe.matmul(
                    pst[:], k["ones_row"][0:1, 0:128],
                    brow[0:1, nh * 512:(nh + 1) * 512], start=False, stop=True))
                c.wait(c.act, "pe", pv)
                c.wait(c.act, "act")
                av = c.inc("act", c.act.activation(
                    out=qc_tok[:, nh * 512:(nh + 1) * 512], in_=pst[:],
                    func=AF.Identity))
                rot.release("act", av)
            for nh in range(2):
                pst = rot.acquire()
                for dc in range(4):
                    c.pe.matmul(pst[:], ctq_t(dc),
                                wqr[:, dc * 1024 + nh * 512: dc * 1024 + nh * 512 + 512],
                                start=(dc == 0), stop=False)
                pv = c.inc("pe", c.pe.matmul(
                    pst[:], k["ones_row"][0:1, 0:128],
                    brow[0:1, 1024 + nh * 512: 1024 + nh * 512 + 512],
                    start=False, stop=True))
                c.wait(c.dve, "act")
                dv = rope_combine(pst, nh, qc_tok, q_tok, pv)
                rot.release("dve", dv)
            for fc in range(8):
                c.wait(c.pe, "dve")
                c.wait(c.pe, "act")
                tv = c.inc("pe", c.pe.transpose(pstr[:],
                                                q_tok[:, fc * 128:(fc + 1) * 128],
                                                k["ident"][:]))
                sk = ("q", fc)
                if sk in st_user:
                    c.wait(c.act, "dmao", st_user[sk])
                c.wait(c.act, "pe", tv)
                c.wait(c.act, "act")
                c.inc("act", c.act.activation(out=QS[fc][:, (tt % 4) * 128:
                                                         (tt % 4) * 128 + 128],
                                              in_=pstr[:], func=AF.Identity))
            c.wait(c.act, "dve")
            for nh in range(2):
                pst = rot.acquire()
                for dc in range(4):
                    c.pe.matmul(pst[:], ctk_t(dc),
                                wuk[:, dc * 1024 + nh * 512: dc * 1024 + nh * 512 + 512],
                                start=(dc == 0), stop=False)
                pv = c.inc("pe", c.pe.matmul(
                    pst[:], k["ones_row"][0:1, 0:128],
                    brow[0:1, 3072 + nh * 512: 3072 + nh * 512 + 512],
                    start=False, stop=True))
                c.wait(c.act, "pe", pv)
                c.wait(c.act, "act")
                av = c.inc("act", c.act.activation(
                    out=qc_tok[:, nh * 512:(nh + 1) * 512], in_=pst[:],
                    func=AF.Identity))
                rot.release("act", av)
            for nh in range(2):
                pst = rot.acquire()
                for cc in range(8):
                    c.pe.matmul(pst[:], ST[cc][:, t0:t0 + 128],
                                wkr[:, cc * 1024 + nh * 512: cc * 1024 + nh * 512 + 512],
                                start=(cc == 0), stop=False)
                pv = c.inc("pe", c.pe.matmul(
                    pst[:], k["ones_row"][0:1, 0:128],
                    brow[0:1, 2048 + nh * 512: 2048 + nh * 512 + 512],
                    start=False, stop=True))
                c.wait(c.dve, "act")
                dv = rope_combine(pst, nh, qc_tok, k_tok, pv)
                rot.release("dve", dv)
            for fc in range(8):
                c.wait(c.pe, "dve")
                c.wait(c.pe, "act")
                tv = c.inc("pe", c.pe.transpose(pstr[:],
                                                k_tok[:, fc * 128:(fc + 1) * 128],
                                                k["ident"][:]))
                sk = ("k", fc)
                if sk in st_user:
                    c.wait(c.act, "dmao", st_user[sk])
                c.wait(c.act, "pe", tv)
                c.wait(c.act, "act")
                c.inc("act", c.act.activation(out=KS[fc][:, (tt % 4) * 128:
                                                         (tt % 4) * 128 + 128],
                                              in_=pstr[:], func=AF.Identity))
            c.wait(c.dve, "dmao", st_user.get("v", 0))
            for nh in range(2):
                pst = rot.acquire()
                for dc in range(4):
                    c.pe.matmul(pst[:], ctk_t(dc),
                                wuv[:, dc * 1024 + nh * 512: dc * 1024 + nh * 512 + 512],
                                start=(dc == 0), stop=False)
                pv = c.inc("pe", c.pe.matmul(
                    pst[:], k["ones_row"][0:1, 0:128],
                    brow[0:1, 4096 + nh * 512: 4096 + nh * 512 + 512],
                    start=False, stop=True))
                c.wait(c.dve, "pe", pv)
                c.wait(c.dve, "dve")
                src3 = pst.ap()[:, 0:512].rearrange("p (h y) -> p h y", y=64)
                dst3 = VS.ap().rearrange("p (h y) -> p h y", y=65)[:, nh * 8:(nh + 1) * 8,
                                                                  0:64]
                dv = c.inc("dve", c.dve.tensor_copy(dst3, src3))
                rot.release("dve", dv)
            c.wait(c.syncq, "dve", dv)
            st_user["v"] = c.store(vt_d[tt], VS[:])
            if tt % 4 == 3:
                grp_pe_user[sl] = c.val("pe")
                grp_dve_user[sl] = c.val("dve")
            if tt % 4 == 3:
                tq = tt // 4
                c.wait(c.syncq, "act")
                for fc in range(8):
                    st_user[("q", fc)] = c.store(
                        qt_d[fc * 128:(fc + 1) * 128, tq * 512:(tq + 1) * 512], QS[fc][:])
                    st_user[("k", fc)] = c.store(
                        kt_d[fc * 128:(fc + 1) * 128, tq * 512:(tq + 1) * 512], KS[fc][:])
        c.wait(c.syncq, "dmao")
    c.phase_barrier()

    with ExitStack() as ph2:
        YT = [c.sb(f"{tag}_yt{i}", (128, T), BF16, ph2) for i in range(8)]
        bo_r = c.sb(f"{tag}_bot", (1, 1024), BF16, ph2)
        c.load_cast(bo_r[:], c.dram["l3a_bo"][:])
        with ExitStack() as ph2a:
            QT = [c.sb(f"{tag}_qt2_{i}", (128, T), BF16, ph2a) for i in range(2)]
            KT = [c.sb(f"{tag}_kt2_{i}", (128, T), BF16, ph2a) for i in range(2)]
            VTile = c.sb(f"{tag}_vt2", (128, NTT * 260), BF16, ph2a)
            a = AttnScratch(c, f"{tag}at", ph2a)
            for g in range(4):
                c.wait(c.pool, "pe")
                lv = 0
                for i in range(2):
                    lv = c.load_cast(QT[i][:],
                                     qt_d[(g * 2 + i) * 128:(g * 2 + i + 1) * 128, :])
                    lv = c.load_cast(KT[i][:],
                                     kt_d[(g * 2 + i) * 128:(g * 2 + i + 1) * 128, :])
                lv = c.load_cast(
                    VTile.ap().rearrange("p (t y) -> p t y", y=260),
                    vt_d.rearrange("t p y -> p t y")[:, :, g * 260:(g + 1) * 260])
                c.wait(c.pe, "dmaw", lv)
                emit_attn_group(c, k, a, g, QT, KT,
                                lambda kt, hh: VTile[:, kt * 260 + hh * 65:
                                                     kt * 260 + hh * 65 + 65],
                                YT, lambda habs, kt: (SC, 0.0), False)
        c.phase_barrier()
        with ExitStack() as phB:
            emit_out_proj_ln(c, k, ST, YT, "l3a_Wo", bo_r, tag, phB)
    return



def emit_router(c, k, ST, CWT, E, topk, rw_slab_name, tag, ph, mode):
    NT = c.NT
    rw_sb = c.sb(f"{tag}_rwsb", (128, 256 if E == 32 else 64), BF16, ph)
    lt = c.sb(f"{tag}_lt", (E, 512), F32, ph)
    LT = c.sb(f"{tag}_LT", (128, E), F32, ph)
    RW = c.sb(f"{tag}_RW", (128, E), F32, ph)
    CA = c.sb(f"{tag}_CA", (128, E), F32, ph)
    WK = c.sb(f"{tag}_WK", (128, E), F32, ph)
    A_ = c.sb(f"{tag}_A", (128, E), F32, ph)
    col = c.sb(f"{tag}_col", (128, 4), F32, ph)
    if mode == "l1":
        ebr = c.sb(f"{tag}_ebr", (128, E), F32, ph)
        c.load_plain(ebr[:], c.dram["l1f_ebr"][:])
    wl = c.load_cast(rw_sb[:], c.dram[rw_slab_name][0])
    bl = c.val("dmas")
    rot = PsRot(c, [c.ps(f"{tag}_psr{i}", stack=ph) for i in range(3)])

    c.wait(c.pe, "dmaw", wl)
    c.wait(c.pe, "dmas", bl)
    for tq in range(NT):
        ts_ = slice(tq * 512, (tq + 1) * 512)
        pst = rot.acquire()
        pv = mm_chunks(c, pst[0:E, :], rw_sb, E, 0, E, ST, ts_)
        c.wait(c.dve, "pe", pv)
        c.wait(c.dve, "dve")
        lv = c.inc("dve", c.dve.tensor_copy(lt[:], pst[0:E, :]))
        rot.release("dve", lv)
        for sub in range(4):
            so = tq * 512 + sub * 128
            c.wait(c.pe, "dve", lv)
            pst = rot.acquire()
            tv = c.inc("pe", c.pe.transpose(pst[0:128, 0:E],
                                            lt[:, sub * 128:(sub + 1) * 128],
                                            k["ident_f32"][0:E, 0:E]))
            c.wait(c.dve, "pe", tv)
            c.wait(c.dve, "dve")
            c.inc("dve", c.dve.tensor_copy(LT[:], pst[0:128, 0:E]))
            c.wait(c.dve, "dve")
            mv = c.inc("dve", c.dve.reduce_max(col[:, 0:1], LT[:], axis=AX.X,
                                               negate=True))
            rot.release("dve", mv)
            c.wait(c.act, "dve", mv)
            c.wait(c.act, "act")
            ev = c.inc("act", c.act.activation(out=RW[:], in_=LT[:], func=AF.Exp,
                                               bias=col[:, 0:1],
                                               accum_out=col[:, 1:2]))
            c.wait(c.dve, "act", ev)
            c.wait(c.dve, "dve")
            c.inc("dve", c.dve.reciprocal(col[:, 2:3], col[:, 1:2]))
            c.wait(c.dve, "dve")
            c.inc("dve", c.dve.tensor_scalar(RW[:], RW[:], col[:, 2:3], None, ALU.mult))
            c.wait(c.dve, "dve")
            if mode == "l1":
                c.inc("dve", c.dve.tensor_tensor(CA[:], RW[:], ebr[:], ALU.add))
            else:
                c.inc("dve", c.dve.tensor_copy(CA[:], RW[:]))
            c.wait(c.dve, "dve")
            c.inc("dve", c.dve.tensor_copy(WK[:], CA[:]))
            for it in range(topk):
                c.wait(c.dve, "dve")
                c.inc("dve", c.dve.reduce_max(col[:, 3:4], WK[:], axis=AX.X))
                if it < topk - 1:
                    c.wait(c.dve, "dve")
                    c.inc("dve", c.dve.tensor_scalar(A_[:], WK[:], col[:, 3:4], None,
                                                     ALU.is_ge))
                    c.wait(c.dve, "dve")
                    c.inc("dve", c.dve.scalar_tensor_tensor(WK[:], A_[:], -1e30, WK[:],
                                                            ALU.mult, ALU.add))
            c.wait(c.dve, "dve")
            c.inc("dve", c.dve.tensor_scalar(A_[:], CA[:], col[:, 3:4], None, ALU.is_ge))
            c.wait(c.dve, "dve")
            c.inc("dve", c.dve.tensor_tensor(A_[:], RW[:], A_[:], ALU.mult))
            c.wait(c.dve, "dve")
            c.inc("dve", c.dve.tensor_reduce(col[:, 0:1], A_[:], axis=AX.X, op=ALU.add))
            c.wait(c.dve, "dve")
            if mode == "l1":
                c.inc("dve", c.dve.tensor_scalar(col[:, 1:2], col[:, 0:1], 1e-12, None,
                                                 ALU.max))
            else:
                c.inc("dve", c.dve.tensor_scalar(col[:, 1:2], col[:, 0:1], 1e-9, None,
                                                 ALU.add))
            c.wait(c.dve, "dve")
            c.inc("dve", c.dve.reciprocal(col[:, 2:3], col[:, 1:2]))
            c.wait(c.dve, "dve")
            cwv = c.inc("dve", c.dve.tensor_scalar(A_[:], A_[:], col[:, 2:3], None,
                                                   ALU.mult))
            c.wait(c.pe, "dve", cwv)
            pst = rot.acquire()
            tv = c.inc("pe", c.pe.transpose(pst[0:E, 0:128], A_[:],
                                            k["ident_f32"][:]))
            c.wait(c.dve, "pe", tv)
            c.wait(c.dve, "dve")
            rot.release("dve", c.inc("dve", c.dve.tensor_copy(
                CWT[:, so:so + 128], pst[0:E, 0:128])))
    return



def declare_l1_ffn(c):
    c.din("l1f_rwT", (1, 128, 8 * 32))
    c.din("l1f_ebr", (128, 32))
    c.din("l1f_gup", (32, 128, 8 * 1024))
    c.din("l1f_down", (32, 128, 8 * 512))
    c.din("l1f_shg", (1, 128, 8 * 1024))
    c.din("l1f_shu", (1, 128, 8 * 1024))
    c.din("l1f_shd", (2, 128, 8 * 512))
    c.din("c_ind32", (32, 32 * 128))


def emit_l1_ffn(c, k, ST, outer):
    T, NT = c.T, c.NT
    tag = "l1f"
    NQ = 4 if T >= 2048 else 1
    TQW = T // NQ
    ntt = TQW // 512
    ind = c.sb(f"{tag}_ind", (32, 128), F32, outer)
    CWTD = c.dtmp(f"{tag}_cwtd", (32, T), F32)

    with ExitStack() as phR:
        CWT = c.sb(f"{tag}_cwt", (32, T), F32, phR)
        emit_router(c, k, ST, CWT, 32, 6, "l1f_rwT", tag + "r", phR, "l1")
        c.wait(c.syncq, "dve")
        c.store(CWTD[:, :], CWT[:])
        c.wait(c.syncq, "dmao")
    c.phase_barrier()

    with ExitStack() as ph:
        OUT = [c.sb(f"{tag}_out{i}", (128, TQW), F32, ph) for i in range(8)]
        gbuf = [c.sb(f"{tag}_gb{i}", (128, 8 * 1024), BF16, ph) for i in range(2)]
        dbuf = [c.sb(f"{tag}_db{i}", (128, 8 * 512), BF16, ph) for i in range(2)]
        HM = [[c.sb(f"{tag}_hm{e}_{i}", (128, TQW), BF16, ph) for i in range(4)]
              for e in range(2)]
        HMS = None
        sg = c.sb(f"{tag}_sg", (128, 512), BF16, ph)
        hm_t = c.sb(f"{tag}_hmt", (128, 512), BF16, ph)
        cwf = c.sb(f"{tag}_cwf", (128, TQW), F32, ph)
        cwq = c.sb(f"{tag}_cwq", (32, TQW), F32, ph)
        rot = PsRot(c, [c.ps(f"{tag}_psr{i}", stack=ph) for i in range(4)])

        gb_user = [0, 0]
        db_user = [0, 0]
        ind_l = c.val("dmas")
        c.wait(c.pe, "dmas", ind_l)
        for q in range(NQ):
            q0 = q * TQW
            cwl = c.load_plain(cwq[:], CWTD[:, q0:q0 + TQW])
            c.wait(c.pe, "dmas", cwl)
            for eg in range(16):
                for e2 in range(2):
                    e = eg * 2 + e2
                    c.wait(c.pool, "pe", gb_user[e2])
                    wg = c.load_cast(gbuf[e2][:], c.dram["l1f_gup"][e])
                    c.wait(c.pool, "pe", db_user[e2])
                    wd = c.load_cast(dbuf[e2][:], c.dram["l1f_down"][e])
                    c.wait(c.pe, "dve")
                    c.wait(c.dve, "pool")
                    c.wait(c.syncq, "pe")
                    iv = c.load_plain(ind[:], c.dram["c_ind32"][:, e * 128:(e + 1) * 128])
                    c.wait(c.pe, "dmas", iv)
                    for tt in range(ntt):
                        pst = rot.acquire()
                        ins = c.pe.matmul(pst[:], ind[:, 0:128],
                                          cwq[0:32, tt * 512:(tt + 1) * 512],
                                          start=True, stop=True)
                        pv = c.inc("pe", ins)
                        c.wait(c.dve, "pe", pv)
                        c.wait(c.dve, "dve")
                        cv = c.inc("dve", c.dve.tensor_copy(
                            cwf[:, tt * 512:(tt + 1) * 512], pst[:]))
                        rot.release("dve", cv)
                    c.wait(c.pe, "dmaw", wg)
                    for isub in range(4):
                        for tt in range(ntt):
                            tsl = slice(q0 + tt * 512, q0 + (tt + 1) * 512)
                            psg = rot.acquire()
                            pvg = mm_chunks(c, psg[:], gbuf[e2], 1024, isub * 128, 128,
                                            ST, tsl)
                            psu = rot.acquire()
                            pvu = mm_chunks(c, psu[:], gbuf[e2], 1024,
                                            512 + isub * 128, 128, ST, tsl)
                            c.wait(c.act, "pe", pvg)
                            c.wait(c.act, "act")
                            c.wait(c.act, "dve")
                            sv = c.inc("act", c.act.activation(out=sg[:], in_=psg[:],
                                                               func=AF.Silu))
                            rot.user[rot.tiles.index(psg)] = ("act", sv)
                            c.wait(c.dve, "act", sv)
                            c.wait(c.dve, "pe", pvu)
                            c.wait(c.dve, "dve")
                            c.wait(c.dve, "pool")
                            hv = c.inc("dve", c.dve.tensor_tensor(hm_t[:], psu[:],
                                                                  sg[:], ALU.mult))
                            rot.user[rot.tiles.index(psu)] = ("dve", hv)
                            c.wait(c.pool, "dve", hv)
                            pv2 = c.inc("pool", c.pool.tensor_tensor(
                                HM[e2][isub][:, tt * 512:(tt + 1) * 512], hm_t[:],
                                cwf[:, tt * 512:(tt + 1) * 512], ALU.mult))
                    gb_user[e2] = c.val("pe")
                c.wait(c.pe, "pool")
                c.wait(c.pe, "dmaw")
                for tt in range(ntt):
                    for oc in range(8):
                        pst = rot.acquire()
                        for e2 in range(2):
                            for ic in range(4):
                                ins = c.pe.matmul(
                                    pst[:],
                                    dbuf[e2][:, ic * 1024 + oc * 128:
                                             ic * 1024 + oc * 128 + 128],
                                    HM[e2][ic][:, tt * 512:(tt + 1) * 512],
                                    start=(e2 == 0 and ic == 0),
                                    stop=(e2 == 1 and ic == 3))
                        pv = c.inc("pe", ins)
                        c.wait(c.dve, "pe", pv)
                        c.wait(c.dve, "dve")
                        if eg == 0:
                            dv = c.inc("dve", c.dve.tensor_copy(
                                OUT[oc][:, tt * 512:(tt + 1) * 512], pst[:]))
                        else:
                            dv = c.inc("dve", c.dve.tensor_tensor(
                                OUT[oc][:, tt * 512:(tt + 1) * 512], pst[:],
                                OUT[oc][:, tt * 512:(tt + 1) * 512], ALU.add))
                        rot.release("dve", dv)
                pe_done = c.val("pe")
                db_user = [pe_done, pe_done]
                gb_user = [pe_done, pe_done]
            c.wait(c.pool, "pe", gb_user[0])
            wsg = c.load_cast(gbuf[0][:], c.dram["l1f_shg"][0])
            c.wait(c.pool, "pe", gb_user[1])
            wsu = c.load_cast(gbuf[1][:], c.dram["l1f_shu"][0])
            c.wait(c.pool, "pe", db_user[0])
            wd0 = c.load_cast(dbuf[0][:], c.dram["l1f_shd"][0])
            c.wait(c.pool, "pe", db_user[1])
            wd1 = c.load_cast(dbuf[1][:], c.dram["l1f_shd"][1])
            c.wait(c.pe, "dmaw", wsu)
            HS = HM[0] + HM[1]
            for isub in range(8):
                for tt in range(ntt):
                    tsl = slice(q0 + tt * 512, q0 + (tt + 1) * 512)
                    psg = rot.acquire()
                    pvg = mm_chunks(c, psg[:], gbuf[0], 1024, isub * 128, 128, ST, tsl)
                    psu = rot.acquire()
                    pvu = mm_chunks(c, psu[:], gbuf[1], 1024, isub * 128, 128, ST, tsl)
                    c.wait(c.act, "pe", pvg)
                    c.wait(c.act, "act")
                    c.wait(c.act, "dve")
                    sv = c.inc("act", c.act.activation(out=sg[:], in_=psg[:],
                                                       func=AF.Silu))
                    rot.user[rot.tiles.index(psg)] = ("act", sv)
                    c.wait(c.dve, "act", sv)
                    c.wait(c.dve, "pe", pvu)
                    c.wait(c.dve, "dve")
                    hv = c.inc("dve", c.dve.tensor_tensor(
                        HS[isub][:, tt * 512:(tt + 1) * 512], psu[:], sg[:], ALU.mult))
                    rot.user[rot.tiles.index(psu)] = ("dve", hv)
            c.wait(c.pe, "dve")
            c.wait(c.pe, "dmaw")
            for tt in range(ntt):
                for oc in range(8):
                    pst = rot.acquire()
                    w_ = dbuf[oc // 4]
                    for ic in range(8):
                        ins = c.pe.matmul(
                            pst[:], w_[:, ic * 512 + (oc % 4) * 128:
                                       ic * 512 + (oc % 4) * 128 + 128],
                            HS[ic][:, tt * 512:(tt + 1) * 512],
                            start=(ic == 0), stop=(ic == 7))
                    pv = c.inc("pe", ins)
                    c.wait(c.dve, "pe", pv)
                    c.wait(c.dve, "dve")
                    dv = c.inc("dve", c.dve.tensor_tensor(
                        OUT[oc][:, tt * 512:(tt + 1) * 512], pst[:],
                        OUT[oc][:, tt * 512:(tt + 1) * 512], ALU.add))
                    rot.release("dve", dv)
            db_user = [c.val("pe"), c.val("pe")]
            gb_user = [c.val("pe"), c.val("pe")]
            c.phase_barrier()
            with ExitStack() as phL:
                ln = LnScratch(c, tag + "ln" + str(q), phL, alloc_r=False)
                for tqq in range(ntt):
                    tq = q * ntt + tqq
                    ts_ = slice(tq * 512, (tq + 1) * 512)
                    Rt = [OUT[oc][:, tqq * 512:(tqq + 1) * 512] for oc in range(8)]
                    for oc in range(8):
                        c.wait(c.dve, "dve")
                        dv = c.inc("dve", c.dve.tensor_tensor(
                            Rt[oc], Rt[oc], ST[oc][:, ts_], ALU.add))
                    emit_ln_tq(c, k, ln, ST, tq, "R_ready", None, R_tiles=Rt)
            c.phase_barrier()
    return



def declare_l2_ffn(c):
    c.din("l2f_WgT", (1, 128, 8 * 8))
    c.din("l2f_W1", (8, 8, 128, 4096))
    c.din("l2f_W2", (8, 8, 2, 128, 2048))
    c.din("l2f_Ws1", (8, 128, 4096))
    c.din("l2f_Ws2", (8, 2, 128, 2048))
    c.din("l2f_b1c", (128, 9 * 32))
    c.din("l2f_b2cat", (1, 9 * 1024))
    c.din("c_ind8", (8, 8 * 128))
    c.din("c_tenth", (1, 512))


def emit_l2_ffn(c, k, ST, outer):
    T, NT = c.T, c.NT
    tag = "l2f"
    NQ = 4 if T >= 2048 else 1
    TQW = T // NQ
    ntt = TQW // 512
    CWTD = c.dtmp("l2f_cwtd", (8, T), F32)

    with ExitStack() as phR:
        CWT = c.sb(f"{tag}_cwtf", (8, T), F32, phR)
        emit_router(c, k, ST, CWT, 8, 2, "l2f_WgT", tag + "r", phR, "l2")
        c.wait(c.syncq, "dve")
        c.store(CWTD[:, :], CWT[:])
        c.wait(c.syncq, "dmao")
    c.phase_barrier()

    with ExitStack() as ph:
        OUT = [c.sb(f"{tag}_out{i}", (128, TQW), F32, ph) for i in range(8)]
        HP = [c.sb(f"{tag}_hp{i}", (128, TQW), BF16, ph) for i in range(16)]
        w1b = [c.sb(f"{tag}_w1b{i}", (128, 4096), BF16, ph) for i in range(2)]
        w2b = [c.sb(f"{tag}_w2b{i}", (128, 2048), BF16, ph) for i in range(2)]
        cwq = c.sb(f"{tag}_cwq", (8, TQW), F32, ph)
        cwf = c.sb(f"{tag}_cwf", (128, TQW), F32, ph)
        cwb16 = c.sb(f"{tag}_cwb16", (1, TQW), BF16, ph)
        ind8 = c.sb(f"{tag}_ind8", (8, 128), F32, ph)
        b1c = c.sb(f"{tag}_b1t", (128, 9 * 32), F32, ph)
        b2t = c.sb(f"{tag}_b2t", (1, 1024), BF16, ph)
        rot = PsRot(c, [c.ps(f"{tag}_psr{i}", stack=ph) for i in range(4)])

        c.load_plain(b1c[:], c.dram["l2f_b1c"][:])
        cst = c.val("dmas")
        cst_w = c.val("dmaw")
        c.wait(c.pe, "dmas", cst)
        c.wait(c.pe, "dmaw", cst_w)
        c.wait(c.act, "dmas", cst)

        w1_user = [0, 0]
        w2_user = [0, 0]
        b2_user = [0]
        for q in range(NQ):
            q0 = q * TQW
            cwl = c.load_plain(cwq[:], CWTD[:, q0:q0 + TQW])
            c.wait(c.pe, "dmas", cwl)
            for e in range(9):
                c.wait(c.dve, "pool")
                if e < 8:
                    c.wait(c.syncq, "pe")
                    iv = c.load_plain(ind8[:], c.dram["c_ind8"][:, e * 128:(e + 1) * 128])
                    c.wait(c.pe, "dmas", iv)
                    for tt in range(ntt):
                        pst = rot.acquire()
                        ins = c.pe.matmul(pst[:], ind8[:, 0:128],
                                          cwq[0:8, tt * 512:(tt + 1) * 512],
                                          start=True, stop=True)
                        pv = c.inc("pe", ins)
                        c.wait(c.dve, "pe", pv)
                        c.wait(c.dve, "dve")
                        cv = c.inc("dve", c.dve.tensor_copy(
                            cwf[:, tt * 512:(tt + 1) * 512], pst[:]))
                        rot.release("dve", cv)
                else:
                    c.wait(c.dve, "dve")
                    cv = c.inc("dve", c.dve.memset(cwf[:], 0.1))
                c.wait(c.dve, "dve")
                cbv = c.inc("dve", c.dve.tensor_copy(cwb16[:], cwf[0:1, :]))
                c.wait(c.pool, "pe", b2_user[0])
                b2l = c.load_cast(b2t[:], c.dram["l2f_b2cat"][:, e * 1024:(e + 1) * 1024])
                for fh in range(2):
                    for fw in range(4):
                        fwa = fh * 4 + fw
                        sl = fw % 2
                        c.wait(c.pool, "pe", w1_user[sl])
                        wsrc = (c.dram["l2f_W1"][e, fwa] if e < 8
                                else c.dram["l2f_Ws1"][fwa])
                        wv_ = c.load_cast(w1b[sl][:], wsrc)
                        c.wait(c.pe, "dmaw", wv_)
                        for fsub in range(4):
                            fc = fw * 4 + fsub
                            fca = fwa * 4 + fsub
                            for tt in range(ntt):
                                tsl = slice(q0 + tt * 512, q0 + (tt + 1) * 512)
                                pst = rot.acquire()
                                pv = mm_chunks(c, pst[:], w1b[sl], 512, fsub * 128, 128,
                                               ST, tsl)
                                c.wait(c.act, "pe", pv)
                                c.wait(c.act, "act")
                                c.wait(c.act, "pool")
                                bcol = e * 32 + fca
                                sv = c.inc("act", c.act.activation(
                                    out=HP[fc][:, tt * 512:(tt + 1) * 512], in_=pst[:],
                                    func=AF.Silu, bias=b1c[:, bcol:bcol + 1]))
                                rot.release("act", sv)
                        w1_user[sl] = c.val("pe")
                    c.wait(c.pool, "act")
                    c.wait(c.pool, "dve", cbv)
                    c.wait(c.pool, "pe")
                    for fc in range(16):
                        for tt in range(ntt):
                            ins = c.pool.tensor_tensor(
                                HP[fc][:, tt * 512:(tt + 1) * 512],
                                HP[fc][:, tt * 512:(tt + 1) * 512],
                                cwf[:, tt * 512:(tt + 1) * 512], ALU.mult)
                            c.inc("pool", ins)
                    pool_hm = c.val("pool")
                    c.wait(c.pe, "pool", pool_hm)
                    c.wait(c.pe, "dmaw", b2l)
                    for oc in range(8):
                        sl = oc % 2
                        c.wait(c.pool, "pe", w2_user[sl])
                        wsrc = (c.dram["l2f_W2"][e, oc, fh] if e < 8
                                else c.dram["l2f_Ws2"][oc, fh])
                        wv_ = c.load_cast(w2b[sl][:], wsrc)
                        c.wait(c.pe, "dmaw", wv_)
                        for tt in range(ntt):
                            pst = rot.acquire()
                            last = (fh == 0)
                            for ic in range(16):
                                ins = c.pe.matmul(
                                    pst[:], w2b[sl][:, ic * 128:(ic + 1) * 128],
                                    HP[ic][:, tt * 512:(tt + 1) * 512],
                                    start=(ic == 0), stop=(last and ic == 15))
                            if fh == 1:
                                ins = c.pe.matmul(
                                    pst[:], b2t[0:1, oc * 128:(oc + 1) * 128],
                                    cwb16[0:1, tt * 512:(tt + 1) * 512],
                                    start=False, stop=True)
                            pv = c.inc("pe", ins)
                            c.wait(c.dve, "pe", pv)
                            c.wait(c.dve, "dve")
                            if e == 0 and fh == 0:
                                dv = c.inc("dve", c.dve.tensor_copy(
                                    OUT[oc][:, tt * 512:(tt + 1) * 512], pst[:]))
                            else:
                                dv = c.inc("dve", c.dve.tensor_tensor(
                                    OUT[oc][:, tt * 512:(tt + 1) * 512], pst[:],
                                    OUT[oc][:, tt * 512:(tt + 1) * 512], ALU.add))
                            rot.release("dve", dv)
                        w2_user[sl] = c.val("pe")
                b2_user[0] = c.val("pe")
            c.phase_barrier()
            with ExitStack() as phL:
                ln = LnScratch(c, tag + "ln" + str(q), phL, alloc_r=False)
                for tqq in range(ntt):
                    tq = q * ntt + tqq
                    ts_ = slice(tq * 512, (tq + 1) * 512)
                    Rt = [OUT[oc][:, tqq * 512:(tqq + 1) * 512] for oc in range(8)]
                    for oc in range(8):
                        c.wait(c.dve, "dve")
                        dv = c.inc("dve", c.dve.tensor_tensor(
                            Rt[oc], Rt[oc], ST[oc][:, ts_], ALU.add))
                    emit_ln_tq(c, k, ln, ST, tq, "R_ready", None, R_tiles=Rt)
            c.phase_barrier()
    return



def declare_l2_attn(c):
    c.din("l2a_W", (12, 128, 8 * 256))
    c.din("l2a_Wo", (2, 128, 8 * 512))
    c.din("l2a_bo", (1, 1024))
    c.din("l2a_proj", (8, 128, 256))


def emit_l2_attn(c, k, ST, outer):
    T, NT, NTT = c.T, c.NT, c.NTT
    tag = "l2"
    YT = [c.sb(f"{tag}_yt{i}", (128, T), BF16, outer) for i in range(8)]
    bo_r = c.sb(f"{tag}_bo", (1, 1024), BF16, outer)
    KVS = c.sb(f"{tag}_kvs", (128, 4 * 130), BF16, outer)

    with ExitStack() as phA:
        QT = [c.sb(f"{tag}_qt{i}", (128, T), BF16, phA) for i in range(2)]
        KT = [c.sb(f"{tag}_kt{i}", (128, T), BF16, phA) for i in range(2)]
        VT = [c.sb(f"{tag}_vt{i}", (128, 260), BF16, phA) for i in range(NTT)]
        wq = c.sb(f"{tag}_wq", (128, 8 * 256), BF16, phA)
        wk = c.sb(f"{tag}_wk", (128, 8 * 256), BF16, phA)
        wv = c.sb(f"{tag}_wv", (128, 8 * 256), BF16, phA)
        prj = c.sb(f"{tag}_prj", (128, 2 * 256), BF16, phA)
        kp = c.sb(f"{tag}_kp", (128, 256), BF16, phA)
        mx = c.sb(f"{tag}_mx", (128, 1), F32, phA)
        P = [c.sb(f"{tag}_P{i}", (128, 512), BF16, phA) for i in range(2)]
        yraw = c.sb(f"{tag}_yraw", (64, 512), BF16, phA)
        prow = c.sb(f"{tag}_prow", (1, 1024), F32, phA)
        rot = PsRot(c, [c.ps(f"{tag}_psr{i}", stack=phA) for i in range(3)])
        ps_kv = [c.ps(f"{tag}_pskv{i}", stack=phA) for i in range(2)]
        ps_y = [c.ps(f"{tag}_psy{i}", stack=phA) for i in range(2)]

        c.load_cast(bo_r[:], c.dram["l2a_bo"][:])

        for tt in range(NTT):
            v3 = VT[tt].ap().rearrange("p (h y) -> p h y", y=65)
            c.inc("dve", c.dve.memset(v3[:, :, 64:65], 1.0))
        vones = c.val("dve")

        qkv_pe_done = 0
        grp_pe_done = 0
        py_user = [None, None]
        for g in range(4):
            c.wait(c.pool, "pe", qkv_pe_done)
            c.load_cast(wq[:], c.dram["l2a_W"][g, :, :])
            c.load_cast(wk[:], c.dram["l2a_W"][4 + g, :, :])
            c.load_cast(wv[:], c.dram["l2a_W"][8 + g, :, :])
            c.wait(c.pool, "pe", grp_pe_done)
            for i in range(2):
                c.load_cast(prj[:, i * 256:(i + 1) * 256],
                            c.dram["l2a_proj"][g * 2 + i])
            wl = c.val("dmaw")
            prv = wl
            c.wait(c.pe, "dmaw", wl)
            for (dst, slab) in ((QT, wq), (KT, wk)):
                for fc in range(2):
                    for tq in range(NT):
                        pstile = rot.acquire()
                        pv = mm_chunks(c, pstile[:], slab, 256, fc * 128, 128,
                                       ST, slice(tq * 512, (tq + 1) * 512))
                        c.wait(c.act, "pe", pv)
                        ins = c.act.activation(out=dst[fc][:, tq * 512:(tq + 1) * 512],
                                               in_=pstile[:], func=AF.Identity)
                        rot.release("act", c.inc("act", ins))
            c.wait(c.pe, "dve", vones)
            for tt in range(NTT):
                pstile = rot.acquire()
                for cc in range(8):
                    ins = c.pe.matmul(pstile[:, 0:256],
                                      ST[cc][:, tt * 128:(tt + 1) * 128],
                                      wv[:, cc * 256:(cc + 1) * 256],
                                      start=(cc == 0), stop=(cc == 7))
                pv = c.inc("pe", ins)
                c.wait(c.dve, "pe", pv)
                c.wait(c.dve, "dve")
                src3 = pstile.ap()[:, 0:256].rearrange("p (h y) -> p h y", y=64)
                dst3 = VT[tt].ap().rearrange("p (h y) -> p h y", y=65)[:, :, 0:64]
                rot.release("dve", c.inc("dve", c.dve.tensor_copy(dst3, src3)))
            qkv_pe_done = c.val("pe")
            c.wait(c.pe, "act")
            c.wait(c.pe, "dve")
            c.wait(c.pe, "dmaw", prv)

            for hh in range(4):
                habs = g * 4 + hh
                pb = (hh % 2) * 64
                ktile = KT[hh // 2]
                prj_t = prj[:, (hh // 2) * 256:(hh // 2) * 256 + 256]
                if py_user[0] is not None:
                    c.wait(c.pe, "dve", py_user[0])
                zk_done = []
                kv_done = []
                for tt in range(NTT):
                    pstile = rot.acquire()
                    ins = c.pe.matmul(pstile[:, 0:256],
                                      ktile[pb:pb + 64, tt * 128:(tt + 1) * 128],
                                      prj_t[pb:pb + 64, :], start=True, stop=True)
                    pv = c.inc("pe", ins)
                    c.wait(c.dve, "pe", pv)
                    c.wait(c.dve, "dve")
                    c.wait(c.dve, "act")
                    mv = c.inc("dve", c.dve.reduce_max(mx[:, 0:1], pstile[:, 0:256],
                                                       axis=AX.X, negate=True))
                    c.wait(c.act, "dve", mv)
                    c.wait(c.act, "act")
                    ev = c.inc("act", c.act.activation(out=kp[:], in_=pstile[:, 0:256],
                                                       func=AF.Exp, bias=mx[:, 0:1]))
                    rot.release("act", ev)
                    c.wait(c.pe, "act", ev)
                    for fs in range(2):
                        ins = c.pe.matmul(ps_kv[fs][:, 0:65],
                                          kp[:, fs * 128:(fs + 1) * 128],
                                          VT[tt][:, hh * 65:hh * 65 + 65],
                                          start=(tt == 0), stop=(tt == NTT - 1))
                    kv_done.append(c.inc("pe", ins))
                c.wait(c.dve, "pe", kv_done[-1])
                c.wait(c.dve, "dve")
                for fs in range(2):
                    dv = c.inc("dve", c.dve.tensor_copy(
                        KVS[:, hh * 130 + fs * 65:hh * 130 + fs * 65 + 65],
                        ps_kv[fs][:, 0:65]))
                py_user[0] = dv

            for hh in range(4):
                habs = g * 4 + hh
                pb = (hh % 2) * 64
                qtile = QT[hh // 2]
                prj_t = prj[:, (hh // 2) * 256:(hh // 2) * 256 + 256]
                for tq in range(NT):
                    py = ps_y[tq % 2]
                    exp_done = []
                    for fs in range(2):
                        pstile = rot.acquire()
                        ins = c.pe.matmul(pstile[:],
                                          prj_t[pb:pb + 64, fs * 128:(fs + 1) * 128],
                                          qtile[pb:pb + 64, tq * 512:(tq + 1) * 512],
                                          start=True, stop=True)
                        pv = c.inc("pe", ins)
                        c.wait(c.act, "pe", pv)
                        c.wait(c.act, "act")
                        ev = c.inc("act", c.act.activation(out=P[fs][:], in_=pstile[:],
                                                           func=AF.Exp))
                        rot.release("act", ev)
                        exp_done.append(ev)
                    c.wait(c.pe, "act", exp_done[-1])
                    c.wait(c.pe, "dve")
                    for fs in range(2):
                        ins = c.pe.matmul(py[0:65, :],
                                          KVS[:, hh * 130 + fs * 65:
                                              hh * 130 + fs * 65 + 65],
                                          P[fs][:], start=(fs == 0), stop=(fs == 1))
                    yv = c.inc("pe", ins)
                    c.wait(c.dve, "pe", yv)
                    c.wait(c.dve, "dve")
                    c.inc("dve", c.dve.tensor_copy(yraw[:], py[0:64, :]))
                    c.wait(c.act, "pe", yv)
                    c.wait(c.act, "act")
                    c.inc("act", c.act.activation(out=prow[:, 0:512],
                                                  in_=py[64:65, :], func=AF.Ln))
                    c.wait(c.act, "act")
                    ar = c.inc("act", c.act.activation(out=prow[:, 512:1024],
                                                       in_=prow[:, 0:512],
                                                       func=AF.Exp, scale=-1.0))
                    c.wait(c.pe, "act", ar)
                    pstile = rot.acquire()
                    bc = c.inc("pe", c.pe.matmul(pstile[0:64, :],
                                                 k["ones_row_f32"][:, 0:64],
                                                 prow[:, 512:1024],
                                                 start=True, stop=True))
                    c.wait(c.dve, "pe", bc)
                    c.wait(c.dve, "dve")
                    ins = c.dve.tensor_tensor(
                        YT[habs // 2][(habs % 2) * 64:(habs % 2) * 64 + 64,
                                      tq * 512:(tq + 1) * 512],
                        yraw[:], pstile[0:64, :], ALU.mult)
                    ydone = c.inc("dve", ins)
                    rot.release("dve", ydone)
                    c.wait(c.pe, "dve", ydone)
            grp_pe_done = c.val("pe")

    c.phase_barrier()
    with ExitStack() as phB:
        emit_out_proj_ln(c, k, ST, YT, "l2a_Wo", bo_r, tag, phB)
    return



def declare_l3_ffn(c):
    c.din("l3f_wfc", (32, 128, 1024))
    c.din("l3f_wproj", (8, 128, 4096))
    c.din("l3f_vT", (1, 128, 8 * 16))
    c.din("l3f_fbT", (16, 16))
    c.din("l3f_uT", (16, 4096))
    c.din("l3f_gate", (1, 1))


def _bitlin_prep(c, k, src_name, n_rc, ncols, dst_name, ph):
    wf = [c.sb(f"{dst_name}_wf{i}", (128, ncols), F32, ph) for i in range(2)]
    sc = c.sb(f"{dst_name}_sc", (128, 4), F32, ph)
    uu = c.sb(f"{dst_name}_u", (128, ncols), F32, ph)
    aa = c.sb(f"{dst_name}_a", (128, ncols), F32, ph)
    weff = c.sb(f"{dst_name}_we", (128, ncols), BF16, ph)
    stg = c.sb(f"{dst_name}_st", (128, ncols), BF16, ph)
    pst = c.ps(f"{dst_name}_pt", (128, 128), BF16, stack=ph)
    nc_ = ncols // 128
    ldv = [0, 0]
    stv = [None, None]
    for rc in range(n_rc):
        sl = rc % 2
        if rc >= 2 and stv[sl] is not None:
            c.wait(c.pool, "dmao", stv[sl])
        ldv[sl] = c.load_cast(wf[sl][:], c.dram[src_name][rc]) if False else \
            c.load_plain(wf[sl][:], c.dram[src_name][rc])
        c.wait(c.dve, "dmas", ldv[sl])
        c.wait(c.dve, "dve")
        c.inc("dve", c.dve.tensor_reduce(sc[:, 0:1], wf[sl][:], axis=AX.X, op=ALU.add,
                                         apply_absolute_value=True))
        c.wait(c.dve, "dve")
        c.inc("dve", c.dve.tensor_scalar(sc[:, 1:2], sc[:, 0:1], 1.0 / ncols, 1e-5,
                                         ALU.mult, ALU.max))
        c.wait(c.dve, "dve")
        c.inc("dve", c.dve.reciprocal(sc[:, 2:3], sc[:, 1:2]))
        c.wait(c.dve, "dve")
        c.inc("dve", c.dve.tensor_scalar(uu[:], wf[sl][:], sc[:, 2:3], None, ALU.mult))
        c.wait(c.dve, "dve")
        c.inc("dve", c.dve.tensor_scalar(aa[:], uu[:], 0.5, None, ALU.is_ge))
        c.wait(c.dve, "dve")
        c.inc("dve", c.dve.tensor_scalar(uu[:], uu[:], -0.5, None, ALU.is_le))
        c.wait(c.dve, "dve")
        c.inc("dve", c.dve.tensor_tensor(aa[:], aa[:], uu[:], ALU.subtract))
        c.wait(c.dve, "dve")
        wev = c.inc("dve", c.dve.tensor_scalar(weff[:], aa[:], sc[:, 1:2], None,
                                               ALU.mult))
        for cc in range(nc_):
            c.wait(c.pe, "dve", wev)
            c.wait(c.pe, "act")
            tv = c.inc("pe", c.pe.transpose(pst[:], weff[:, cc * 128:(cc + 1) * 128],
                                            k["ident"][:]))
            c.wait(c.act, "pe", tv)
            c.wait(c.act, "act")
            c.inc("act", c.act.activation(out=stg[:, cc * 128:(cc + 1) * 128],
                                          in_=pst[:], func=AF.Identity))
        c.wait(c.syncq, "act")
        stv[sl] = c.store(c.dram[dst_name][rc], stg[:])
        c.wait(c.act, "dmao", stv[sl])
    c.wait(c.pool, "dmao")


def emit_l3_ffn(c, k, ST, outer):
    T, NT = c.T, c.NT
    nic = 4096 // 128
    wfcT = c.dtmp("l3f_wfcT", (32, 128, 1024), BF16)
    wprojT = c.dtmp("l3f_wprojT", (8, 128, 4096), BF16)

    with ExitStack() as ph0:
        _bitlin_prep(c, k, "l3f_wfc", 32, 1024, "l3f_wfcT", ph0)
        _bitlin_prep(c, k, "l3f_wproj", 8, 4096, "l3f_wprojT", ph0)
    c.phase_barrier()

    with ExitStack() as ph:
        vT = c.sb("l3f_vTt", (128, 8 * 16), BF16, ph)
        fbT = c.sb("l3f_fbTf", (16, 16), F32, ph)
        fbs = c.sb("l3f_fbs", (16, 16), BF16, ph)
        uT = c.sb("l3f_uTt", (16, 4096), BF16, ph)
        gt = c.sb("l3f_gt", (1, 1), F32, ph)
        g16 = c.sb("l3f_g16", (16, 1), F32, ph)
        XV = c.sb("l3f_xv", (16, T), BF16, ph)
        XVB = c.sb("l3f_xvb", (16, T), BF16, ph)
        wp = [c.sb(f"l3f_wp{i}", (128, 4096), BF16, ph) for i in range(2)]
        wfr = [c.sb(f"l3f_wfr{i}", (128, 1024), BF16, ph) for i in range(2)]
        H = [c.sb(f"l3f_h{i}", (128, 512), BF16, ph) for i in range(nic)]
        rel = c.sb("l3f_rel", (128, 512), F32, ph)
        ln = LnScratch(c, "l3fln", ph)
        rot = PsRot(c, [c.ps(f"l3f_psr{i}", stack=ph) for i in range(3)])

        c.load_cast(vT[:], c.dram["l3f_vT"][0])
        c.load_cast(uT[:], c.dram["l3f_uT"][:])
        c.load_plain(fbT[:], c.dram["l3f_fbT"][:])
        c.load_plain(gt[:], c.dram["l3f_gate"][:])
        wl = c.val("dmaw")
        bl = c.val("dmas")

        c.wait(c.pe, "dmas", bl)
        pst = rot.acquire()
        bc = c.inc("pe", c.pe.matmul(pst[0:16, 0:1], k["ones_row_f32"][:, 0:16],
                                     gt[:, 0:1], start=True, stop=True))
        c.wait(c.dve, "pe", bc)
        c.wait(c.dve, "dve")
        c.inc("dve", c.dve.tensor_copy(g16[:], pst[0:16, 0:1]))
        c.wait(c.dve, "dve")
        fbv = c.inc("dve", c.dve.tensor_scalar(fbs[:], fbT[:], g16[:, 0:1], None,
                                               ALU.mult))
        rot.release("dve", fbv)

        c.wait(c.pe, "dmaw", wl)
        for tq in range(NT):
            ts_ = slice(tq * 512, (tq + 1) * 512)
            pst = rot.acquire()
            pv = mm_chunks(c, pst[0:16, :], vT, 16, 0, 16, ST, ts_)
            c.wait(c.dve, "pe", pv)
            c.wait(c.dve, "dve")
            rot.release("dve", c.inc("dve", c.dve.tensor_copy(XV[:, ts_], pst[0:16, :])))
        c.wait(c.pe, "dve")
        for tq in range(NT):
            ts_ = slice(tq * 512, (tq + 1) * 512)
            pst = rot.acquire()
            pv = c.inc("pe", c.pe.matmul(pst[0:16, :], fbs[:], XV[:, ts_],
                                         start=True, stop=True))
            c.wait(c.dve, "pe", pv)
            c.wait(c.dve, "dve")
            rot.release("dve", c.inc("dve", c.dve.tensor_copy(XVB[:, ts_], pst[0:16, :])))

        wfr_user = [0, 0]
        wp_user = [0, 0]
        for tq in range(NT):
            ts_ = slice(tq * 512, (tq + 1) * 512)
            c.wait(c.pe, "dve")
            for ic in range(nic):
                sl = ic % 2
                c.wait(c.pool, "pe", wfr_user[sl])
                wv_ = c.load_cast(wfr[sl][:], c.dram["l3f_wfcT"][ic])
                pst = rot.acquire()
                c.wait(c.pe, "dmaw", wv_)
                for cc in range(8):
                    c.pe.matmul(pst[:], wfr[sl][:, cc * 128:(cc + 1) * 128],
                                ST[cc][:, ts_], start=(cc == 0), stop=False)
                pv = c.inc("pe", c.pe.matmul(pst[:], uT[:, ic * 128:(ic + 1) * 128],
                                             XVB[:, ts_], start=False, stop=True))
                wfr_user[sl] = pv
                c.wait(c.dve, "pe", pv)
                c.wait(c.dve, "dve")
                c.wait(c.dve, "act")
                rv = c.inc("dve", c.dve.tensor_scalar(rel[:], pst[:], 0.0, None,
                                                      ALU.max))
                c.wait(c.act, "dve", rv)
                c.wait(c.act, "act")
                hv = c.inc("act", c.act.activation(out=H[ic][:], in_=rel[:],
                                                   func=AF.Square))
                rot.release("act", hv)
            c.wait(c.pe, "act")
            for oc in range(8):
                sl = oc % 2
                c.wait(c.pool, "pe", wp_user[sl])
                wv_ = c.load_cast(wp[sl][:], c.dram["l3f_wprojT"][oc])
                pst = rot.acquire()
                c.wait(c.pe, "dmaw", wv_)
                pv = mm_chunks(c, pst[:], wp[sl], 128, 0, 128, H, slice(0, 512),
                               nchunks=nic)
                wp_user[sl] = pv
                c.wait(c.dve, "pe", pv)
                c.wait(c.dve, "dve")
                dv = c.inc("dve", c.dve.tensor_tensor(ln.R[oc][:], pst[:],
                                                      ST[oc][:, ts_], ALU.add))
                rot.release("dve", dv)
            emit_ln_tq(c, k, ln, ST, tq, "R_ready", None)
    return



BLOCKS = {
    "l0_attn": (declare_l0_attn, emit_l0_attn),
    "l0_ffn": (declare_l0_ffn, emit_l0_ffn),
    "l1_attn": (declare_l1_attn, emit_l1_attn),
    "l1_ffn": (declare_l1_ffn, emit_l1_ffn),
    "l3_attn": (declare_l3_attn, emit_l3_attn),
    "l2_ffn": (declare_l2_ffn, emit_l2_ffn),
    "l2_attn": (declare_l2_attn, emit_l2_attn),
    "l3_ffn": (declare_l3_ffn, emit_l3_ffn),
}


def build_program(T, blocks, load_kind, store_kind):
    nc = bass.Bass("TRN2", target_bir_lowering=False, debug=False)
    with ExitStack() as stack:
        c = Ctx(nc, T, stack)
        c.mksem("pe", "act", "dve", "pool", "dmaw", "dmas", "dmao")
        block = stack.enter_context(nc.Block())

        @block.sync
        def _(eng):
            ST = [c.sb(f"ST{i}", (128, T), BF16) for i in range(8)]
            k = setup_consts(c)
            if load_kind == "x":
                c.din("xT", (1024, T), F32)
                emit_load_stream(c, ST, "xT", F32)
            else:
                c.din("stream_in", (1024, T), BF16)
                emit_load_stream(c, ST, "stream_in", BF16)
            for e in (c.pe, c.dve, c.act):
                c.wait(e, "dmaw")
                c.wait(e, "dmas")

            for b in blocks:
                decl, emit = BLOCKS[b]
                decl(c)
                with ExitStack() as bstack:
                    emit(c, k, ST, bstack)
                c.phase_barrier()

            if store_kind == "stream":
                c.dout("stream_out", (1024, T), BF16)
                c.wait(c.syncq, "dve")
                emit_store_stream_mid(c, ST, "stream_out")
            elif store_kind == "final":
                out_ap = c.dout("out_f32", (1024, T), F32)
                fstage = c.sb("fstage", (128, T), F32)
                c.phase_barrier()
                for cc in range(8):
                    if cc > 0:
                        c.wait(c.dve, "dmao")
                    c.wait(c.dve, "dve")
                    dv = c.inc("dve", c.dve.tensor_copy(fstage[:], ST[cc][:]))
                    c.wait(c.syncq, "dve", dv)
                    c.store(out_ap[cc * 128:(cc + 1) * 128, :], fstage[:])
            c.wait(c.syncq, "dmao")
    return nc, c.input_names


CHAIN = ["l0_attn", "l0_ffn", "l1_attn", "l1_ffn", "l2_attn", "l2_ffn",
         "l3_attn", "l3_ffn"]

BLOCK_GROUPS = [CHAIN]


def kernel(**inputs):
    T = T_FULL
    x = np.asarray(inputs["L0_attn_x"])
    shared = host_prep(inputs, T)
    stream = None
    for gi, grp in enumerate(BLOCK_GROUPS):
        first = (gi == 0)
        last = (gi == len(BLOCK_GROUPS) - 1)
        nc, names = build_program(T, grp, "x" if first else "stream",
                                  "final" if last else "stream")
        in_maps = []
        for core in range(N_CORES):
            m = {}
            for n in names:
                if n == "xT":
                    m[n] = prep_x(x[core])
                elif n == "stream_in":
                    m[n] = stream[core]
                else:
                    m[n] = shared[n]
            in_maps.append(m)
        res = run_bass_kernel_spmd(nc, in_maps, core_ids=list(range(N_CORES)))
        if last:
            outs = [np.asarray(r["out_f32"]).T for r in res.results]
        else:
            stream = [np.ascontiguousarray(np.asarray(r["stream_out"]))
                      for r in res.results]
    return np.ascontiguousarray(np.stack(outs))
